# version 10
# baseline (speedup 1.0000x reference)
import math
import numpy as np
import ml_dtypes
import concourse.bass as bass
import concourse.mybir as mybir
from concourse.ap import AP
from concourse.bass_utils import run_bass_kernel_spmd

F32 = mybir.dt.float32
BF16 = mybir.dt.bfloat16
AF = mybir.ActivationFunctionType
ALU = mybir.AluOpType
AXX = mybir.AxisListType.X

NC = 8
T = 8192
TL = T // NC
D = 2048
NH = 32
DCQ = 1536
DC = 512
E = 8
INV = 1.0 / math.sqrt(128.0)
SHIFT = 40.0

R_KC = 0
R_KR = 2048
R_M = 2112
R_V = 2115
NROWS = 4163
N1 = NROWS * 1024
CC1_ROWS = 2115

ENG_SEM = {"t": "PE", "v": "VE", "s": "SE", "g": "PO"}


class Builder:
    def __init__(self):
        self.streams = {e: [] for e in ("t", "v", "s", "g", "y")}
        self.cnt = {}
        self.res = {}

    def _inc(self, name, by):
        self.cnt[name] = self.cnt.get(name, 0) + by
        return (name, self.cnt[name])

    def op(self, eng, fn, reads=(), writes=(), extra=(), dma="dW", sem=None,
           quiet=False):
        own = ENG_SEM.get(eng)
        waits = [w for w in extra if w]
        for r in reads:
            e_ = self.res.get(r)
            if e_ and e_[0]:
                waits.append(e_[0])
        for r in writes:
            e_ = self.res.get(r)
            if e_:
                if e_[0]:
                    waits.append(e_[0])
                waits.extend(e_[1])
        if eng == "t":
            waits = [w for w in waits if w[0] != "PE"]
        if quiet:
            self.streams[eng].append((fn, tuple(waits), None, 0))
            return None
        if sem is None:
            sem = dma if eng == "y" else own
        by = 16 if eng == "y" else 1
        tok = self._inc(sem, by)
        self.streams[eng].append((fn, tuple(waits), sem, by))
        for r in reads:
            self.res.setdefault(r, [None, []])[1].append(tok)
        for r in writes:
            self.res[r] = [tok, []]
        return tok

    def groupify(self, sem):
        tot = (sem, self.cnt.get(sem, 0))
        for r, e_ in self.res.items():
            if e_[0] and e_[0][0] == sem:
                e_[0] = tot
            e_[1] = [tot if t[0] == sem else t for t in e_[1]]

    def emit(self, block, hs):
        def mk(ename):
            waited = {}

            def _run(eng):
                for fn, waits, sem, by in self.streams[ename]:
                    for (s, v) in waits:
                        if waited.get(s, 0) < v:
                            eng.wait_ge(hs[s], v)
                            waited[s] = v
                    ins = fn(eng)
                    if sem is not None:
                        ins.then_inc(hs[sem], by)
            return _run
        block.tensor(mk("t"))
        block.vector(mk("v"))
        block.scalar(mk("s"))
        block.gpsimd(mk("g"))
        block.sync(mk("y"))


def build_program(stage=7):
    nc = bass.Bass("TRN2", target_bir_lowering=False)
    b = Builder()

    def din(name, shape, dt=F32):
        return nc.dram_tensor(name, shape, dt, kind="ExternalInput")

    xT = din("xT", [D, TL])
    wdq = din("wdq", [12, 128, 2048])
    wuq = din("wuq", [16, 128, DCQ])
    wdkv = din("wdkv", [4, 128, 2048])
    wuk = din("wuk", [16, 128, DC])
    wuv = din("wuv", [4, 128, 2048])
    wo = din("wo", [16, 128, 2048])
    expw = din("expw", [E, 16, 128, 2048])
    wqr = din("wqr", [128, 768])
    wqrs = din("wqrs", [128, 768])
    wkr = din("wkr", [128, 1024])
    wkrs = din("wkrs", [128, 1024])
    gwH = din("gwH", [128, 128])
    gbH = din("gbH", [8, 1])
    ebH = din("ebH", [128, 128])
    qnH = din("qnH", [128, 12])
    kvnH = din("kvnH", [128, 4])
    identH = din("identH", [128, 128])
    onesfH = din("onesfH", [128, 128])
    onesbH = din("onesbH", [128, 128], BF16)
    fconsH = din("fconsH", [128, 12])
    ropeCH = din("ropeCH", [64, TL], BF16)
    ropeSH = din("ropeSH", [64, TL], BF16)

    out = nc.dram_tensor("out", [D, TL], F32, kind="ExternalOutput")
    dbg = {}
    if stage < 7 or stage == 8:
        dbg["actA"] = nc.dram_tensor("dbg_actA", [128, 16 * 1024], BF16,
                                     kind="ExternalOutput")
        dbg["mask"] = nc.dram_tensor("dbg_mask", [8, 1024], F32,
                                     kind="ExternalOutput")
        dbg["xres"] = nc.dram_tensor("dbg_xres", [128, 16 * 1024], F32,
                                     kind="ExternalOutput")
        if stage >= 4:
            dbg["cq"] = nc.dram_tensor("dbg_cq", [128, 12 * 1024], BF16,
                                       kind="ExternalOutput")
            dbg["stg"] = nc.dram_tensor("dbg_stg", [NROWS, 1024], BF16,
                                        kind="ExternalOutput")
        if stage == 8:
            dbg["k0"] = nc.dram_tensor("dbg_k0", [128, 4096], BF16,
                                       kind="ExternalOutput")
            dbg["k1"] = nc.dram_tensor("dbg_k1", [128, 4096], BF16,
                                       kind="ExternalOutput")
            dbg["qt"] = nc.dram_tensor("dbg_qt", [128, 4096], BF16,
                                       kind="ExternalOutput")
            dbg["pt"] = nc.dram_tensor("dbg_pt", [128, 2048], BF16,
                                       kind="ExternalOutput")
            dbg["bias"] = nc.dram_tensor("dbg_bias", [128, 4], F32,
                                         kind="ExternalOutput")
            dbg["vs"] = nc.dram_tensor("dbg_vs", [128, 4160], BF16,
                                       kind="ExternalOutput")
            dbg["ao"] = nc.dram_tensor("dbg_ao", [64, 1024], BF16,
                                       kind="ExternalOutput")
            dbg["ao1"] = nc.dram_tensor("dbg_ao1", [64, 1024], BF16,
                                        kind="ExternalOutput")
            dbg["k2"] = nc.dram_tensor("dbg_k2", [128, 4096], BF16,
                                       kind="ExternalOutput")

    stage_in = nc.dram_tensor("stage_in", [N1], BF16)
    C1N = CC1_ROWS * 1024
    C2N = (NROWS - CC1_ROWS) * 1024
    stage_out = nc.dram_tensor("stage_out", [NC * C1N], BF16, addr_space="Shared")
    stage_out2 = nc.dram_tensor("stage_out2", [NC * C2N], BF16, addr_space="Shared")

    def sba(name, shape, dt):
        return nc.alloc_sbuf_tensor(name, shape, dt).ap()

    xres = sba("xres", [128, 16 * 1024], F32)
    actA = sba("actA", [128, 16 * 1024], BF16)
    cq = sba("cq", [128, 12 * 1024], BF16)
    shAt = nc.alloc_sbuf_tensor("shA", [128, 24704], BF16)
    shA = shAt.ap()
    shAf = shA.bitcast(F32)
    qt = sba("qt", [128, 4 * 1024], BF16)
    pt = sba("pt", [128, 2 * 1024], BF16)
    qsq = sba("qsq", [128, 1024], BF16)
    wbf = sba("wbf", [128, 2 * 2048], BF16)
    onesb = sba("onesb", [128, 128], BF16)
    mglob = sba("mglob", [128, 24], BF16)
    msq = sba("msq", [128, 24], BF16)
    mloc = sba("mloc", [128, 24], BF16)
    mgat = sba("mgat", [128, 192], BF16)
    wf32 = sba("wf32", [128, 1024], F32)
    scr1 = sba("scr1", [128, 1024], F32)
    scr2 = sba("scr2", [128, 1024], F32)
    ebT = sba("ebT", [128, 128], F32)
    gwT = sba("gwT", [128, 128], F32)
    gbT = sba("gbT", [8, 1], F32)
    qnT = sba("qnT", [128, 12], F32)
    kvnT = sba("kvnT", [128, 4], F32)
    identT = sba("identT", [128, 128], F32)
    onesf = sba("onesf", [128, 128], F32)
    fcons = sba("fcons", [128, 12], F32)
    sc = sba("sc", [128, 16], F32)
    biasb = sba("biasb", [128, 4], F32)
    gtm = sba("gtm", [128, 64], F32)
    selm = sba("selm", [128, 64], F32)
    gscr = sba("gscr", [128, 64], F32)
    fscr = sba("fscr", [128, 24], F32)

    def bfv(off, plen, cols):
        return shA[0:plen, off:off + cols]

    def f32v(off, plen, cols):
        return shAf[0:plen, off // 2:off // 2 + cols]

    def U(off, cols):
        return [f"u{k}" for k in range(off // 1024, (off + cols + 1023) // 1024)]

    rowm = lambda: f32v(0, 1, 1024)
    K_rowm = U(0, 2048)
    rowsq = lambda: f32v(2048, 1, 1024)
    K_rowsq = U(2048, 2048)
    rowt = lambda: f32v(4096, 1, 1024)
    K_rowt = U(4096, 2048)
    glrow = lambda: f32v(6144, 8, 1024)
    K_glrow = U(6144, 2048)
    rowe = lambda: f32v(8192, 1, 1024)
    K_rowe = U(8192, 2048)
    maskbv = lambda s: bfv(10240 + s * 1024, 128, 1024)
    K_maskb = lambda s: U(10240 + s * 1024, 1024)
    qslot = lambda s: bfv(12288 + s * 1024, 128, 1024)
    K_qslot = lambda s: U(12288 + s * 1024, 1024)
    nrmT = lambda s: f32v(12288 + s * 2048, 128, 1024)
    NRM_KEYS = [U(12288, 2048), U(14336, 2048)]
    ropeC = lambda: bfv(0, 64, 1024)
    K_ropeC = U(0, 1024)
    ropeS = lambda: bfv(1024, 64, 1024)
    K_ropeS = U(1024, 1024)
    kropev = lambda: bfv(2048, 64, 1024)
    K_krope = U(2048, 1024)
    kconv = lambda s: bfv(3072 + s * 1024, 128, 1024)
    K_kcon = lambda s: U(3072 + s * 1024, 1024)
    qropev = lambda: bfv(5120, 64, 1024)
    K_qrope = U(5120, 1024)
    wqrb = lambda: bfv(6144, 128, 768)
    K_wqrb = U(6144, 768)
    wqrsb = lambda: bfv(6912, 128, 768)
    K_wqrsb = U(6912, 768)
    wkrb = lambda: bfv(7680, 128, 1024)
    K_wkrb = U(7680, 1024)
    wkrsb = lambda: bfv(8704, 128, 1024)
    K_wkrsb = U(8704, 1024)
    ckvv = lambda: bfv(9728, 128, 4096)
    K_ckv = U(9728, 4096)
    rowP = lambda: f32v(13824, 1, 1024)
    ROWP_KEYS = U(13824, 2048)
    tmphv = lambda s: bfv(14336 + s * 1024, 64, 1024)
    K_tmph = lambda s: U(14336 + s * 1024, 1024)
    kring = lambda s: bfv(s * 4096, 128, 4096)
    wbig = lambda i: f32v(16384 + i * 4160, 128, 2048)
    K_kring = lambda s: U(s * 4096, 4096)
    vslot = lambda s: bfv(16384 + s * 4160, 128, 4160)
    K_vslot = lambda s: [f"vs{s}"]

    pA = nc.alloc_psum_tensor("pA", [128, 2048], F32).ap()
    pB = nc.alloc_psum_tensor("pB", [128, 1024], F32).ap()
    pE = nc.alloc_psum_tensor("pE", [128, 1024], F32).ap()

    XR = [f"xr{c}" for c in range(16)]
    AC = [f"ac{c}" for c in range(16)]

    for (d_, s_) in [(ebT, ebH), (gwT, gwH), (gbT, gbH), (qnT, qnH),
                     (kvnT, kvnH), (identT, identH), (onesf, onesfH),
                     (onesb, onesbH), (fcons, fconsH)]:
        b.op("y", lambda e, d=d_, s=s_: e.dma_start(out=d, in_=s.ap()),
             writes=["K"], dma="dC")
    b.op("y", lambda e: e.dma_start(
        out=xres.rearrange("p (c t) -> p c t", t=1024),
        in_=xT.ap().rearrange("(c p) t -> p c t", p=128)),
        writes=XR, dma="dC")
    b.groupify("dC")
    b.op("g", lambda e: e.memset(mloc[:, :], 0.0), writes=["mloc"])

    _wb = {"i": 0}

    def wstream_big(srcap, ncols, slot_ap, slot_key, eng=None):
        assert not isinstance(slot_key, list)
        hp = _wb["i"] % 2
        _wb["i"] += 1
        wk = f"vs{hp}"
        b.op("y", lambda e, src=srcap, n=ncols, hp=hp: e.dma_start(
            out=wbig(hp)[:, 0:n], in_=src), writes=[wk], dma=f"dF{hp}")
        h = ncols // 2
        b.op("g", lambda e, h=h, sl=slot_ap, hp=hp: e.tensor_copy(
            out=sl[:, 0:h], in_=wbig(hp)[:, 0:h]),
            reads=[wk], writes=[slot_key + "a"])
        b.op("v", lambda e, h=h, n=ncols, sl=slot_ap, hp=hp: e.tensor_copy(
            out=sl[:, h:n], in_=wbig(hp)[:, h:n]),
            reads=[wk], writes=[slot_key + "b"])

    def wstream(srcaps, ncols_each, slot_ap, slot_key, eng="g"):
        keys = slot_key if isinstance(slot_key, list) else [slot_key + "a"]
        coff = 0
        for srcap, ncols in zip(srcaps, ncols_each):
            assert ncols <= 512, ncols
            hp = (coff // 512) % 2
            wk = f"wf32{hp}"
            b.op("y", lambda e, s=srcap, n=ncols, hp=hp: e.dma_start(
                out=wf32[:, hp * 512:hp * 512 + n], in_=s),
                writes=[wk], dma=f"dF{hp}")
            b.op(eng, lambda e, n=ncols, co=coff, sl=slot_ap, hp=hp: e.tensor_copy(
                out=sl[:, co:co + n], in_=wf32[:, hp * 512:hp * 512 + n]),
                reads=[wk], writes=keys)
            coff += ncols

    def bcast_cols(row_ap, row_key, dst_ap, dst_key, psum_key="pA1"):
        keys = row_key if isinstance(row_key, list) else [row_key]
        for qb in range(2):
            b.op("t", lambda e, qb=qb: e.matmul(
                pA[:, 1024 + qb * 512:1024 + (qb + 1) * 512],
                onesf[0:1, 0:128], row_ap[0:1, qb * 512:(qb + 1) * 512],
                start=True, stop=True), reads=keys + ["K"], writes=[psum_key])
        b.op("v", lambda e: e.tensor_copy(out=dst_ap, in_=pA[:, 1024:2048]),
             reads=[psum_key], writes=[dst_key])

    def emit_ln(gate):
        for c in range(16):
            b.op("v", lambda e, c=c: e.tensor_copy(
                out=actA[:, c * 1024:(c + 1) * 1024],
                in_=xres[:, c * 1024:(c + 1) * 1024]),
                reads=[XR[c], "K"], writes=[AC[c]])
        for c in range(16):
            for qb in range(2):
                b.op("t", lambda e, c=c, qb=qb: e.matmul(
                    pE[0:1, qb * 512:(qb + 1) * 512], onesb[:, 0:1],
                    actA[:, c * 1024 + qb * 512:c * 1024 + (qb + 1) * 512],
                    start=(c == 0), stop=(c == 15)),
                    reads=[AC[c], "K"], writes=["pE"])
            b.op("v", lambda e, c=c: e.tensor_tensor(
                out=qslot(c % 2), in0=actA[:, c * 1024:(c + 1) * 1024],
                in1=actA[:, c * 1024:(c + 1) * 1024], op=ALU.mult),
                reads=[AC[c]], writes=K_qslot(c % 2))
            for qb in range(2):
                b.op("t", lambda e, c=c, qb=qb: e.matmul(
                    pB[0:1, qb * 512:(qb + 1) * 512], onesb[:, 0:1],
                    qslot(c % 2)[:, qb * 512:(qb + 1) * 512],
                    start=(c == 0), stop=(c == 15)),
                    reads=K_qslot(c % 2) + ["K"], writes=[f"pB{qb}"])
        b.op("v", lambda e: e.tensor_scalar(
            out=rowm(), in0=pE[0:1, 0:1024], scalar1=fcons[0:1, 5:6],
            scalar2=None, op0=ALU.mult), reads=["pE", "K"], writes=K_rowm)
        b.op("v", lambda e: e.tensor_scalar(
            out=rowsq(), in0=pB[0:1, 0:1024], scalar1=fcons[0:1, 5:6],
            scalar2=None, op0=ALU.mult), reads=["pB0", "pB1", "K"],
            writes=K_rowsq)
        b.op("v", lambda e: e.tensor_tensor(
            out=rowt(), in0=rowm(), in1=rowm(), op=ALU.mult),
            reads=K_rowm, writes=K_rowt)
        b.op("v", lambda e: e.tensor_tensor(
            out=rowsq(), in0=rowsq(), in1=rowt(), op=ALU.subtract),
            reads=K_rowt, writes=K_rowsq)
        b.op("s", lambda e: e.activation(
            out=rowt(), in_=rowsq(), func=AF.Ln, bias=fcons[0:1, 3:4]),
            reads=K_rowsq + ["K"], writes=K_rowt)
        b.op("s", lambda e: e.activation(
            out=rowsq(), in_=rowt(), func=AF.Exp, scale=-0.5),
            reads=["rowt"], writes=K_rowsq)
        bcast_cols(rowm(), K_rowm, scr1[:, :], "scr1")
        bcast_cols(rowsq(), K_rowsq, scr2[:, :], "scr2")
        for c in range(16):
            nk = NRM_KEYS[c % 2]
            b.op("v", lambda e, c=c: e.tensor_tensor(
                out=nrmT(c % 2), in0=xres[:, c * 1024:(c + 1) * 1024],
                in1=scr1[:, :], op=ALU.subtract),
                reads=[XR[c], "scr1"], writes=nk)
            if gate:
                b.op("v", lambda e, c=c: e.tensor_tensor(
                    out=nrmT(c % 2), in0=nrmT(c % 2), in1=scr2[:, :],
                    op=ALU.mult), reads=["scr2"], writes=nk)
                for qb in range(2):
                    b.op("t", lambda e, c=c, qb=qb: e.matmul(
                        pA[0:8, qb * 512:(qb + 1) * 512],
                        gwT[:, c * 8:(c + 1) * 8],
                        nrmT(c % 2)[:, qb * 512:(qb + 1) * 512],
                        start=(c == 0), stop=(c == 15)),
                        reads=nk + ["K"], writes=["pA0"])
                b.op("v", lambda e, c=c: e.tensor_copy(
                    out=actA[:, c * 1024:(c + 1) * 1024], in_=nrmT(c % 2)),
                    reads=nk, writes=[AC[c]])
            else:
                b.op("v", lambda e, c=c: e.tensor_tensor(
                    out=actA[:, c * 1024:(c + 1) * 1024], in0=nrmT(c % 2),
                    in1=scr2[:, :], op=ALU.mult),
                    reads=nk + ["scr2"], writes=[AC[c]])
        if not gate:
            return
        b.op("s", lambda e: e.activation(
            out=glrow(), in_=pA[0:8, 0:1024], func=AF.Identity,
            bias=gbT[0:8, 0:1]), reads=["pA0", "K"], writes=K_glrow)
        for i in range(8):
            b.op("t", lambda e, i=i: e.transpose(
                pB[0:128, i * 8:(i + 1) * 8],
                glrow()[0:8, i * 128:(i + 1) * 128], identT[0:8, 0:8]),
                reads=K_glrow + ["K"], writes=["pB0"])
        b.op("v", lambda e: e.tensor_copy(out=gtm[:, :], in_=pB[:, 0:64]),
             reads=["pB0"], writes=["gtm"])
        b.op("v", lambda e: e.reduce_max(
            out=fscr[:, 0:8], in_=gtm[:, :].rearrange("p (a b) -> p a b", b=8),
            axis=AXX), reads=["gtm"], writes=["fscr1"])
        for i in range(8):
            b.op("v", lambda e, i=i: e.tensor_scalar(
                out=selm[:, i * 8:(i + 1) * 8], in0=gtm[:, i * 8:(i + 1) * 8],
                scalar1=fscr[:, i:i + 1], scalar2=None, op0=ALU.is_equal),
                reads=["gtm", "fscr1"], writes=["selm"])
        b.op("v", lambda e: e.tensor_scalar(
            out=gscr[:, :], in0=selm[:, :], scalar1=fcons[0:128, 1:2],
            scalar2=None, op0=ALU.mult), reads=["selm", "K"], writes=["gscr"])
        b.op("v", lambda e: e.tensor_tensor(
            out=gscr[:, :], in0=gtm[:, :], in1=gscr[:, :], op=ALU.subtract),
            reads=["gtm"], writes=["gscr"])
        b.op("v", lambda e: e.reduce_max(
            out=fscr[:, 8:16], in_=gscr[:, :].rearrange("p (a b) -> p a b", b=8),
            axis=AXX), reads=["gscr"], writes=["fscr2"])
        for i in range(8):
            b.op("v", lambda e, i=i: e.tensor_scalar(
                out=selm[:, i * 8:(i + 1) * 8], in0=gtm[:, i * 8:(i + 1) * 8],
                scalar1=fscr[:, 8 + i:9 + i], op0=ALU.is_ge,
                scalar2=fcons[0:128, 2:3], op1=ALU.mult),
                reads=["gtm", "fscr2", "K"], writes=["selm"])
        for i in range(8):
            b.op("t", lambda e, i=i: e.transpose(
                pB[0:8, i * 128:(i + 1) * 128], selm[:, i * 8:(i + 1) * 8],
                identT[:, :]), reads=["selm", "K"],
                writes=["pB0" if i < 4 else "pB1"])
        b.op("v", lambda e: e.tensor_copy(
            out=scr2[0:8, :], in_=pB[0:8, 0:1024]),
            reads=["pB0", "pB1"], writes=["scr2"])

    def emit_ffn(final=False):
        for e_ in range(E):
            sl = e_ % 2
            b.op("y", lambda e, e_=e_: e.dma_start(
                out=rowe(), in_=scr2[e_:e_ + 1, :]),
                reads=["scr2"], writes=K_rowe, dma="dR")
            for qb in range(2):
                b.op("t", lambda e, qb=qb: e.matmul(
                    pE[:, qb * 512:(qb + 1) * 512], onesf[0:1, 0:128],
                    rowe()[0:1, qb * 512:(qb + 1) * 512], start=True, stop=True),
                    reads=K_rowe + ["K"], writes=["pE"])
            b.op("v", lambda e, sl=sl: e.tensor_copy(
                out=maskbv(sl), in_=pE[:, 0:1024]),
                reads=["pE"], writes=K_maskb(sl))
            for c in range(16):
                idx = e_ * 16 + c
                ip = idx % 2
                wstream_big(expw.ap()[e_, c, :, :],
                    2048, wbf[:, ip * 2048:(ip + 1) * 2048], f"wb{ip}")
                for kk in range(16):
                    for qb in range(2):
                        last = (kk == 15 and qb == 1)
                        b.op("t", lambda e, kk=kk, qb=qb, ip=ip: e.matmul(
                            pA[:, ip * 1024 + qb * 512:ip * 1024 + (qb + 1) * 512],
                            wbf[:, ip * 2048 + kk * 128:ip * 2048 + (kk + 1) * 128],
                            actA[:, kk * 1024 + qb * 512:kk * 1024 + (qb + 1) * 512],
                            start=(kk == 0), stop=(kk == 15)),
                            reads=[f"wb{ip}a", f"wb{ip}b"] + (AC if last or kk == 0 else []),
                            writes=[f"pA{ip}"] if (last or (kk == 0 and qb == 0)) else [],
                            quiet=not last)
                b.op("s", lambda e, idx=idx, ip=ip: e.activation(
                    out=qslot(ip), in_=pA[:, ip * 1024:(ip + 1) * 1024],
                    func=AF.Relu, bias=ebT[:, idx % 128:idx % 128 + 1]),
                    reads=[f"pA{ip}", "K"], writes=K_qslot(ip))
                b.op("v", lambda e, ip=ip, sl=sl: e.tensor_tensor(
                    out=pt[:, ip * 1024:(ip + 1) * 1024], in0=qslot(ip),
                    in1=maskbv(sl), op=ALU.mult),
                    reads=K_qslot(ip) + K_maskb(sl), writes=[f"pt{ip}"])
                b.op("v", lambda e, c=c, ip=ip: e.tensor_tensor(
                    out=xres[:, c * 1024:(c + 1) * 1024],
                    in0=xres[:, c * 1024:(c + 1) * 1024],
                    in1=pt[:, ip * 1024:(ip + 1) * 1024], op=ALU.add),
                    reads=[f"pt{ip}"], writes=[XR[c]])
                if final and e_ == E - 1:
                    b.op("y", lambda e, c=c: e.dma_start(
                        out=out.ap()[c * 128:(c + 1) * 128, :],
                        in_=xres[:, c * 1024:(c + 1) * 1024]),
                        reads=[XR[c]], dma="SO")

    def emit_debug(items):
        for fn, reads in items:
            b.op("y", lambda e, fn=fn: fn(e), reads=reads, dma="SO")
        fin = ("SO", b.cnt.get("SO", 0))
        b.op("y", lambda e: e.nop(), extra=[fin], dma="dC")

    dbg_actA = lambda: (lambda e: e.dma_start(out=dbg["actA"].ap(), in_=actA), AC)
    dbg_mask = lambda: (lambda e: e.dma_start(out=dbg["mask"].ap(), in_=scr2[0:8, :]), ["scr2"])
    dbg_xres = lambda: (lambda e: e.dma_start(out=dbg["xres"].ap(), in_=xres), XR)

    emit_ln(gate=True)
    if stage == 1:
        emit_debug([dbg_actA(), dbg_mask(), dbg_xres()])
        return nc, b
    emit_ffn()
    if stage == 2:
        emit_debug([dbg_actA(), dbg_mask(), dbg_xres()])
        return nc, b
    emit_ln(gate=False)
    if stage == 3:
        emit_debug([dbg_actA(), dbg_xres()])
        return nc, b

    b.op("y", lambda e: e.dma_start(out=ropeC(), in_=ropeCH.ap()),
         writes=K_ropeC, dma="dQ")
    b.op("y", lambda e: e.dma_start(out=ropeS(), in_=ropeSH.ap()),
         writes=K_ropeS, dma="dQ")
    b.groupify("dQ")
    wstream([wqr.ap()[:, 0:512], wqr.ap()[:, 512:768]],
            [512, 256], wqrb(), K_wqrb)
    wstream([wqrs.ap()[:, 0:512], wqrs.ap()[:, 512:768]],
            [512, 256], wqrsb(), K_wqrsb)
    wstream([wkr.ap()[:, 0:512], wkr.ap()[:, 512:1024]],
            [512, 512], wkrb(), K_wkrb)
    wstream([wkrs.ap()[:, 0:512], wkrs.ap()[:, 512:1024]],
            [512, 512], wkrsb(), K_wkrsb)
    b.res.setdefault("glrow", [None, []])
    b.res.setdefault("rowe", [None, []])

    def rmsnorm_block(nchunks, wsrc, dest_fn, dest_key, gamma_col_fn, scol,
                      dest_units=None, weng="g"):
        for c in range(nchunks):
            ip = c % 2
            wstream_big(wsrc.ap()[c, :, :],
                2048, wbf[:, ip * 2048:(ip + 1) * 2048], f"wb{ip}")
            for kk in range(16):
                for qb in range(2):
                    last = (kk == 15 and qb == 1)
                    b.op("t", lambda e, kk=kk, qb=qb, ip=ip: e.matmul(
                        pA[:, ip * 1024 + qb * 512:ip * 1024 + (qb + 1) * 512],
                        wbf[:, ip * 2048 + kk * 128:ip * 2048 + (kk + 1) * 128],
                        actA[:, kk * 1024 + qb * 512:kk * 1024 + (qb + 1) * 512],
                        start=(kk == 0), stop=(kk == 15)),
                        reads=[f"wb{ip}a", f"wb{ip}b"] + (AC if last or kk == 0 else []),
                        writes=[f"pA{ip}"] if (last or (kk == 0 and qb == 0)) else [],
                        quiet=not last)
            du = dest_units(c) if dest_units else []
            b.op("v", lambda e, c=c, ip=ip: e.tensor_copy(
                out=dest_fn(c), in_=pA[:, ip * 1024:(ip + 1) * 1024]),
                reads=[f"pA{ip}"], writes=[f"{dest_key}{c}"] + du)
            b.op("v", lambda e, c=c, ip=ip: e.tensor_tensor(
                out=pt[:, ip * 1024:(ip + 1) * 1024], in0=dest_fn(c),
                in1=dest_fn(c), op=ALU.mult),
                reads=[f"{dest_key}{c}"], writes=[f"pt{ip}"])
            for qb in range(2):
                b.op("t", lambda e, c=c, qb=qb, ip=ip, n=nchunks: e.matmul(
                    pE[0:1, qb * 512:(qb + 1) * 512], onesb[:, 0:1],
                    pt[:, ip * 1024 + qb * 512:ip * 1024 + (qb + 1) * 512],
                    start=(c == 0), stop=(c == n - 1)),
                    reads=[f"pt{ip}", "K"], writes=["pE"])
        b.op("v", lambda e, scol=scol: e.tensor_scalar(
            out=rowP(), in0=pE[0:1, 0:1024], scalar1=fcons[0:1, scol:scol + 1],
            scalar2=None, op0=ALU.mult), reads=["pE", "K"], writes=ROWP_KEYS)
        b.op("s", lambda e: e.activation(
            out=scr1[0:1, :], in_=rowP(), func=AF.Ln, bias=fcons[0:1, 4:5]),
            reads=ROWP_KEYS + ["K"], writes=["scr1"])
        b.op("s", lambda e: e.activation(
            out=rowP(), in_=scr1[0:1, :], func=AF.Exp, scale=-0.5),
            reads=["scr1"], writes=ROWP_KEYS)
        for qb in range(2):
            b.op("t", lambda e, qb=qb: e.matmul(
                pA[:, 1024 + qb * 512:1024 + (qb + 1) * 512],
                onesf[0:1, 0:128], rowP()[0:1, qb * 512:(qb + 1) * 512],
                start=True, stop=True), reads=ROWP_KEYS + ["K"], writes=["pA1"])
        b.op("v", lambda e: e.tensor_copy(out=scr2[:, :], in_=pA[:, 1024:2048]),
             reads=["pA1"], writes=["scr2"])
        for c in range(nchunks):
            du = dest_units(c) if dest_units else []
            b.op("v", lambda e, c=c: e.tensor_tensor(
                out=dest_fn(c), in0=dest_fn(c), in1=scr2[:, :], op=ALU.mult),
                reads=["scr2"], writes=[f"{dest_key}{c}"] + du)
            b.op("v", lambda e, c=c: e.tensor_scalar(
                out=dest_fn(c), in0=dest_fn(c), scalar1=gamma_col_fn(c),
                scalar2=None, op0=ALU.mult), reads=["K"],
                writes=[f"{dest_key}{c}"] + du)

    def rope_build(wbk, wsbk, wb_ap, wsb_ap, nkk, rhs_fn, rhs_keys, dest_ap,
                   dest_keys):
        for kk in range(nkk):
            for qb in range(2):
                b.op("t", lambda e, kk=kk, qb=qb: e.matmul(
                    pA[0:64, qb * 512:(qb + 1) * 512],
                    wb_ap[:, kk * 64:(kk + 1) * 64],
                    rhs_fn(kk)[:, qb * 512:(qb + 1) * 512],
                    start=(kk == 0), stop=(kk == nkk - 1)),
                    reads=wbk + [rhs_keys[kk]], writes=["pA0"])
        for kk in range(nkk):
            for qb in range(2):
                b.op("t", lambda e, kk=kk, qb=qb: e.matmul(
                    pA[0:64, 1024 + qb * 512:1024 + (qb + 1) * 512],
                    wsb_ap[:, kk * 64:(kk + 1) * 64],
                    rhs_fn(kk)[:, qb * 512:(qb + 1) * 512],
                    start=(kk == 0), stop=(kk == nkk - 1)),
                    reads=wsbk + [rhs_keys[kk]], writes=["pA1"])
        b.op("v", lambda e: e.tensor_tensor(
            out=scr1[0:64, :], in0=pA[0:64, 0:1024], in1=ropeC(), op=ALU.mult),
            reads=["pA0"] + K_ropeC, writes=["scr1"])
        b.op("v", lambda e: e.tensor_tensor(
            out=scr2[0:64, :], in0=pA[0:64, 1024:2048], in1=ropeS(),
            op=ALU.mult), reads=["pA1"] + K_ropeS, writes=["scr2"])
        b.op("v", lambda e, d=dest_ap: e.tensor_tensor(
            out=d, in0=scr1[0:64, :], in1=scr2[0:64, :], op=ALU.add),
            reads=["scr1", "scr2"], writes=dest_keys)

    rmsnorm_block(4, wdkv, lambda c: ckvv()[:, c * 1024:(c + 1) * 1024], "ck",
                  lambda c: kvnT[:, c:c + 1], 7,
                  dest_units=lambda c: U(9728 + c * 1024, 1024))

    for c in range(16):
        ip = c % 2
        wstream_big(wuk.ap()[c, :, :],
                    512, wbf[:, ip * 2048:ip * 2048 + 512], f"wb{ip}")
        for kk in range(4):
            for qb in range(2):
                b.op("t", lambda e, kk=kk, qb=qb, ip=ip: e.matmul(
                    pA[:, ip * 1024 + qb * 512:ip * 1024 + (qb + 1) * 512],
                    wbf[:, ip * 2048 + kk * 128:ip * 2048 + (kk + 1) * 128],
                    ckvv()[:, kk * 1024 + qb * 512:kk * 1024 + (qb + 1) * 512],
                    start=(kk == 0), stop=(kk == 3)),
                    reads=[f"wb{ip}a", f"wb{ip}b", f"ck{kk}"] + U(9728 + kk * 1024, 1024),
                    writes=[f"pA{ip}"])
        b.op("v", lambda e, ip=ip: e.tensor_copy(
            out=kconv(ip), in_=pA[:, ip * 1024:(ip + 1) * 1024]),
            reads=[f"pA{ip}"], writes=K_kcon(ip))
        b.op("v", lambda e, c=c, ip=ip: e.reduce_max(
            out=mloc[:, c:c + 1], in_=kconv(ip), axis=AXX,
            apply_absolute_value=True), reads=K_kcon(ip), writes=["mloc"])
        b.op("y", lambda e, c=c, ip=ip: e.dma_start(
            out=AP(tensor=stage_in, offset=(R_KC + c * 128) * 1024,
                   ap=[[1024, 128], [1, 1024]]),
            in_=kconv(ip)), reads=K_kcon(ip) + ["stgin1"], dma="dS%d" % ip)

    rope_build(K_wkrb, K_wkrsb, wkrb(), wkrsb(), 16,
               lambda kk: actA[:, kk * 1024:(kk + 1) * 1024],
               AC, kropev(), K_krope)
    b.op("v", lambda e: e.reduce_max(
        out=mloc[0:64, 16:17], in_=kropev(), axis=AXX,
        apply_absolute_value=True), reads=K_krope, writes=["mloc"])
    b.op("y", lambda e: e.dma_start(
        out=AP(tensor=stage_in, offset=R_KR * 1024, ap=[[1024, 64], [1, 1024]]),
        in_=kropev()), reads=K_krope + ["stgin1"], dma="dM")
    b.op("y", lambda e: e.dma_start(
        out=AP(tensor=stage_in, offset=R_M * 1024, ap=[[24, 128], [1, 24]]),
        in_=mloc[:, :]), reads=["mloc", "stgin1"], dma="dM")
    b.groupify("dM")

    cc1 = b.op("g", lambda e: e.collective_compute(
        "AllGather", ALU.bypass, replica_groups=[list(range(NC))],
        ins=[stage_in.ap()[0:CC1_ROWS * 1024]
             .rearrange("(a b) -> a b", b=1024).opt()],
        outs=[stage_out.ap().rearrange("(a b) -> a b", b=1024).opt()]),
        writes=["stgin1", "stgout"], sem="CC")

    for r in range(NC):
        b.op("y", lambda e, r=r: e.dma_start(
            out=mgat[:, r * 24:(r + 1) * 24],
            in_=AP(tensor=stage_out, offset=r * C1N + R_M * 1024,
                   ap=[[24, 128], [1, 24]])),
            reads=["stgout"], writes=["mgat"], dma="dM")
    b.groupify("dM")
    b.op("v", lambda e: e.reduce_max(
        out=mglob[:, :], in_=mgat[:, :].rearrange("p (r c) -> p c r", c=24),
        axis=AXX), reads=["mgat"], writes=["mglob"])
    b.op("v", lambda e: e.tensor_tensor(
        out=msq[:, :], in0=mglob[:, :], in1=mglob[:, :], op=ALU.mult),
        reads=["mglob"], writes=["msq"])

    for ob in range(4):
        ip = ob % 2
        wstream_big(wuv.ap()[ob, :, :],
                    2048, wbf[:, ip * 2048:(ip + 1) * 2048], f"wb{ip}")
        for tt in range(8):
            tp = tt % 2
            for dc in range(4):
                b.op("t", lambda e, tt=tt, dc=dc, ip=ip, tp=tp: e.matmul(
                    pB[:, tp * 512:(tp + 1) * 512],
                    ckvv()[:, dc * 1024 + tt * 128:dc * 1024 + tt * 128 + 128],
                    wbf[:, ip * 2048 + dc * 512:ip * 2048 + (dc + 1) * 512],
                    start=(dc == 0), stop=(dc == 3)),
                    reads=[f"wb{ip}a", f"wb{ip}b", f"ck{dc}"] + U(9728 + dc * 1024, 1024),
                    writes=[f"pB{tp}"])
            b.op("v", lambda e, tp=tp: e.tensor_copy(
                out=pt[:, tp * 1024:tp * 1024 + 512],
                in_=pB[:, tp * 512:(tp + 1) * 512]),
                reads=[f"pB{tp}"], writes=[f"pt{tp}"])
            b.op("y", lambda e, tt=tt, ob=ob, tp=tp: e.dma_start(
                out=AP(tensor=stage_in,
                       offset=R_V * 1024 + tt * 128 * 2048 + ob * 512,
                       ap=[[2048, 128], [1, 512]]),
                in_=pt[:, tp * 1024:tp * 1024 + 512]),
                reads=[f"pt{tp}", "stgin2"], dma="dS%d" % (2 + tp))

    cc2 = b.op("g", lambda e: e.collective_compute(
        "AllGather", ALU.bypass, replica_groups=[list(range(NC))],
        ins=[stage_in.ap()[CC1_ROWS * 1024:]
             .rearrange("(a b) -> a b", b=1024).opt()],
        outs=[stage_out2.ap().rearrange("(a b) -> a b", b=1024).opt()]),
        writes=["stgin2", "stgout2"], sem="CC")
    rmsnorm_block(12, wdq, lambda c: cq[:, c * 1024:(c + 1) * 1024], "cq",
                  lambda c: qnT[:, c:c + 1], 6, weng="v")
    rope_build(K_wqrb, K_wqrsb, wqrb(), wqrsb(), 12,
               lambda kk: cq[:, kk * 1024:(kk + 1) * 1024],
               [f"cq{k}" for k in range(12)], qropev(), K_qrope)

    for s_ in range(4):
        r0 = 64 if s_ % 2 == 0 else 0
        b.op("y", lambda e, s_=s_, r0=r0: e.dma_start(
            out=qt[r0:r0 + 64, s_ * 1024:(s_ + 1) * 1024], in_=qropev()),
            reads=K_qrope, writes=[f"qt{s_}r"], dma="dQ")

    b.groupify("dQ")


    if stage == 4:
        emit_debug([
            ((lambda e: e.dma_start(out=dbg["cq"].ap(), in_=cq)),
             [f"cq{k}" for k in range(12)]),
            ((lambda e: e.dma_start(
                out=dbg["stg"].ap(),
                in_=stage_in.ap().rearrange("(a b) -> a b", b=1024))),
             ["stgin1", "stgin2"]),
            dbg_actA(), dbg_xres(),
        ])
        return nc, b

    ATTN_HEADS = 2 if stage == 8 else NH

    def emit_k_half(g):
        if g >= 2 * ATTN_HEADS:
            return
        h, half = g // 2, g % 2
        slot = g % 3
        crow = 0 if h % 2 == 0 else 64
        rrow = 64 - crow
        for r in range(4 * half, 4 * half + 4):
            b.op("y", lambda e, r=r, h=h, slot=slot, crow=crow: e.dma_start(
                out=kring(slot)[crow:crow + 64,
                                (r % 4) * 1024:(r % 4 + 1) * 1024],
                in_=AP(tensor=stage_out,
                       offset=r * C1N + (R_KC + h * 64) * 1024,
                       ap=[[1024, 64], [1, 1024]])),
                reads=["stgout"], writes=K_kring(slot), dma="dK%d" % slot)
            b.op("y", lambda e, r=r, slot=slot, rrow=rrow: e.dma_start(
                out=kring(slot)[rrow:rrow + 64,
                                (r % 4) * 1024:(r % 4 + 1) * 1024],
                in_=AP(tensor=stage_out, offset=r * C1N + R_KR * 1024,
                       ap=[[1024, 64], [1, 1024]])),
                reads=["stgout"], writes=K_kring(slot), dma="dK%d" % slot)

    def emit_v_load(h):
        if h >= ATTN_HEADS:
            return
        sl = h % 2
        b.op("g", lambda e, sl=sl: e.memset(vslot(sl), 1.0),
             writes=K_vslot(sl))
        for r in range(NC):
            b.op("y", lambda e, r=r, h=h, sl=sl: e.dma_start(
                out=vslot(sl)[:, r * 520:(r + 1) * 520].rearrange(
                    "p (k j) -> p k j", j=65)[:, :, 0:64],
                in_=AP(tensor=stage_out2, offset=r * C2N + h * 64,
                       ap=[[2048, 128], [128 * 2048, 8], [1, 64]])),
                reads=["stgout2"], writes=K_vslot(sl), dma="dV%d" % sl)

    def emit_qpair_wstream(p):
        ip = p % 2
        wstream(
            [wuq.ap()[p, :, q * 512:(q + 1) * 512] for q in range(3)],
            [512] * 3, wbf[:, ip * 2048:ip * 2048 + 1536], f"wb{ip}", eng="v")

    def emit_qpair(p):
        ip = p % 2
        cqk = [f"cq{k}" for k in range(12)]
        for kk in range(12):
            for qb in range(2):
                last = (kk == 11 and qb == 1)
                b.op("t", lambda e, kk=kk, qb=qb, ip=ip: e.matmul(
                    pE[:, qb * 512:(qb + 1) * 512],
                    wbf[:, ip * 2048 + kk * 128:ip * 2048 + (kk + 1) * 128],
                    cq[:, kk * 1024 + qb * 512:kk * 1024 + (qb + 1) * 512],
                    start=(kk == 0), stop=(kk == 11)),
                    reads=[f"wb{ip}a", f"wb{ip}b"] + (cqk if last or kk == 0 else []),
                    writes=["pE"] if (last or (kk == 0 and qb == 0)) else [],
                    quiet=not last)
        s0, s1 = (2 * p) % 4, (2 * p + 1) % 4
        b.op("v", lambda e, s0=s0: e.tensor_copy(
            out=qt[0:64, s0 * 1024:(s0 + 1) * 1024], in_=pE[0:64, 0:1024]),
            reads=["pE", f"qt{s0}r"], writes=[f"qt{s0}"])
        b.op("v", lambda e, s1=s1: e.tensor_copy(
            out=qt[64:128, s1 * 1024:(s1 + 1) * 1024], in_=pE[64:128, 0:1024]),
            reads=["pE", f"qt{s1}r"], writes=[f"qt{s1}"])

    def emit_bias(h):
        s_ = h % 4
        b.op("v", lambda e, s_=s_: e.tensor_tensor(
            out=qsq[:, :], in0=qt[:, s_ * 1024:(s_ + 1) * 1024],
            in1=qt[:, s_ * 1024:(s_ + 1) * 1024], op=ALU.mult),
            reads=[f"qt{s_}", f"qt{s_}r"], writes=["qsq"])
        for qb in range(2):
            b.op("t", lambda e, qb=qb: e.matmul(
                pE[0:1, qb * 512:(qb + 1) * 512], onesb[:, 0:1],
                qsq[:, qb * 512:(qb + 1) * 512], start=True, stop=True),
                reads=["qsq", "K"], writes=["pE"])
        b.op("v", lambda e: e.reduce_max(
            out=sc[0:1, 0:1], in_=pE[0:1, 0:512], axis=AXX),
            reads=["pE"], writes=["sc"])
        b.op("v", lambda e: e.reduce_max(
            out=sc[0:1, 1:2], in_=pE[0:1, 512:1024], axis=AXX),
            reads=["pE"], writes=["sc"])
        b.op("v", lambda e: e.tensor_tensor(
            out=sc[0:1, 2:3], in0=sc[0:1, 0:1], in1=sc[0:1, 1:2], op=ALU.max),
            writes=["sc"])
        cb = (h % 2) * 64
        b.op("t", lambda e, cb=cb, h=h: e.matmul(
            pE[0:1, 0:1], onesb[cb:cb + 64, 0:1],
            msq[cb:cb + 64, h // 2:h // 2 + 1], start=True, stop=False),
            reads=["msq", "K", "sc"], writes=["pE"])
        b.op("t", lambda e: e.matmul(
            pE[0:1, 0:1], onesb[0:64, 0:1], msq[0:64, 16:17],
            start=False, stop=True), reads=["msq", "K"], writes=["pE"])
        b.op("v", lambda e: e.tensor_tensor(
            out=sc[0:1, 3:4], in0=pE[0:1, 0:1], in1=sc[0:1, 2:3], op=ALU.mult),
            reads=["pE"], writes=["sc"])
        b.op("s", lambda e: e.activation(
            out=sc[0:1, 4:5], in_=sc[0:1, 3:4], func=AF.Ln),
            reads=["sc"], writes=["sc2"])
        b.op("s", lambda e: e.activation(
            out=sc[0:1, 5:6], in_=sc[0:1, 4:5], func=AF.Exp, scale=0.5,
            bias=fcons[0:1, 0:1]), reads=["sc2", "K"], writes=["sc3"])
        b.op("v", lambda e: e.tensor_scalar(
            out=sc[0:1, 6:7], in0=sc[0:1, 5:6], scalar1=fcons[0:1, 8:9],
            op0=ALU.mult, scalar2=fcons[0:1, 9:10], op1=ALU.add),
            reads=["sc3", "K"], writes=["sc4"])
        b.op("t", lambda e: e.matmul(
            pE[:, 0:1], onesf[0:1, 0:128], sc[0:1, 6:7], start=True, stop=True),
            reads=["sc4", "K"], writes=["pE"])
        b.op("v", lambda e, h=h: e.tensor_copy(
            out=biasb[:, h % 4:h % 4 + 1], in_=pE[:, 0:1]),
            reads=["pE"], writes=[f"bb{h % 4}"])

    def emit_finalize(h):
        sl = h % 2
        b.op("v", lambda e: e.reciprocal(
            out=scr1[64:65, 0:1024], in_=pB[64:65, 0:1024]),
            reads=["pB0", "pB1"], writes=["scr1"])
        for qb in range(2):
            b.op("t", lambda e, qb=qb: e.matmul(
                pE[0:64, qb * 512:(qb + 1) * 512], onesf[64:65, 0:64],
                scr1[64:65, qb * 512:(qb + 1) * 512], start=True, stop=True),
                reads=["scr1", "K"], writes=["pE"])
        b.op("v", lambda e: e.tensor_copy(
            out=scr1[0:64, 0:1024], in_=pE[0:64, 0:1024]),
            reads=["pE"], writes=["scr1"])
        b.op("v", lambda e, sl=sl: e.tensor_tensor(
            out=tmphv(sl), in0=pB[0:64, 0:1024],
            in1=scr1[0:64, 0:1024], op=ALU.mult),
            reads=["pB0", "pB1", "scr1"], writes=K_tmph(sl))
        b.op("y", lambda e, h=h, sl=sl: e.dma_start(
            out=actA[(h % 2) * 64:(h % 2) * 64 + 64,
                     (h // 2) * 1024:(h // 2 + 1) * 1024],
            in_=tmphv(sl)),
            reads=K_tmph(sl), writes=[AC[h // 2]], dma="dH%d" % sl)

    for g_ in (0, 1, 2):
        emit_k_half(g_)
    emit_v_load(0)
    emit_v_load(1)
    emit_qpair_wstream(0)
    emit_qpair(0)
    emit_bias(0)
    emit_bias(1)
    if ATTN_HEADS > 1:
        emit_qpair_wstream(1)

    for h in range(ATTN_HEADS):
        s_ = h % 4
        sl = h % 2
        def s_ap(kc):
            m = kc % 3
            return pE[:, 0:1024] if m == 2 else pA[:, m * 1024:(m + 1) * 1024]

        def s_key(kc):
            m = kc % 3
            return "pE" if m == 2 else f"pA{m}"

        def emit_pv(kc):
            ph = kc % 2
            for qb in range(2):
                b.op("t", lambda e, kc=kc, qb=qb, ph=ph, sl=sl: e.matmul(
                    pB[0:65, qb * 512:(qb + 1) * 512],
                    vslot(sl)[:, kc * 65:kc * 65 + 65],
                    pt[:, ph * 1024 + qb * 512:ph * 1024 + (qb + 1) * 512],
                    start=(kc == 0), stop=(kc == 63)),
                    reads=[f"pt{ph}"] + K_vslot(sl),
                    writes=[f"pB{qb}"] if qb == 1 else ["pB0"],
                    quiet=(qb == 0))

        for kc in range(64):
            g_ = 2 * h + kc // 32
            slot = g_ % 3
            ph = kc % 2
            for qb in range(2):
                b.op("t", lambda e, kc=kc, qb=qb, slot=slot, s_=s_, sa=s_ap(kc): e.matmul(
                    sa[:, qb * 512:(qb + 1) * 512],
                    kring(slot)[:, (kc % 32) * 128:(kc % 32 + 1) * 128],
                    qt[:, s_ * 1024 + qb * 512:s_ * 1024 + (qb + 1) * 512],
                    start=True, stop=True),
                    reads=K_kring(slot) + [f"qt{s_}", f"qt{s_}r"],
                    writes=[s_key(kc)], quiet=(qb == 0))
            if kc >= 2:
                emit_pv(kc - 2)
            b.op("s", lambda e, ph=ph, s4=s_, sa=s_ap(kc): e.activation(
                out=pt[:, ph * 1024:(ph + 1) * 1024],
                in_=sa, func=AF.Exp,
                bias=biasb[:, s4:s4 + 1], scale=INV),
                reads=[s_key(kc), f"bb{s_}"], writes=[f"pt{ph}"])
            if kc == 31:
                emit_k_half(2 * h + 3)
            if h % 2 == 1 and stage != 8:
                p_ = (h + 1) // 2
                if kc == 8 and p_ < 16:
                    emit_qpair(p_)
                if kc == 24 and p_ < 16:
                    emit_bias(h + 1)
                if kc == 40 and p_ < 16:
                    emit_bias(h + 2)
                if kc == 48 and p_ + 1 < 16:
                    emit_qpair_wstream(p_ + 1)
        emit_pv(62)
        emit_pv(63)
        emit_k_half(2 * h + 4)
        emit_finalize(h)
        emit_v_load(h + 2)

    if stage == 8:
        emit_debug([
            ((lambda e: e.dma_start(out=dbg["k0"].ap(), in_=kring(0))),
             K_kring(0)),
            ((lambda e: e.dma_start(out=dbg["k1"].ap(), in_=kring(1))),
             K_kring(1)),
            ((lambda e: e.dma_start(out=dbg["qt"].ap(), in_=qt[:, 0:4096])),
             ["qt0", "qt0r", "qt1", "qt1r"]),
            ((lambda e: e.dma_start(out=dbg["pt"].ap(), in_=pt)),
             ["pt0", "pt1"]),
            ((lambda e: e.dma_start(out=dbg["bias"].ap(), in_=biasb)),
             ["bb0", "bb1"]),
            ((lambda e: e.dma_start(out=dbg["vs"].ap(), in_=vslot(0))),
             K_vslot(0)),
            ((lambda e: e.dma_start(out=dbg["ao"].ap(), in_=tmphv(0))),
             K_tmph(0)),
            ((lambda e: e.dma_start(out=dbg["ao1"].ap(), in_=tmphv(1))),
             K_tmph(1)),
            ((lambda e: e.dma_start(out=dbg["k2"].ap(), in_=kring(2))),
             K_kring(2)),
        ])
        return nc, b

    for c in range(16):
        ip = c % 2
        wstream_big(wo.ap()[c, :, :],
            2048, wbf[:, ip * 2048:(ip + 1) * 2048], f"wb{ip}")
        for kk in range(16):
            for qb in range(2):
                last = (kk == 15 and qb == 1)
                b.op("t", lambda e, kk=kk, qb=qb, ip=ip: e.matmul(
                    pA[:, ip * 1024 + qb * 512:ip * 1024 + (qb + 1) * 512],
                    wbf[:, ip * 2048 + kk * 128:ip * 2048 + (kk + 1) * 128],
                    actA[:, kk * 1024 + qb * 512:kk * 1024 + (qb + 1) * 512],
                    start=(kk == 0), stop=(kk == 15)),
                    reads=[f"wb{ip}a", f"wb{ip}b"] + (AC if last or kk == 0 else []),
                    writes=[f"pA{ip}"] if (last or (kk == 0 and qb == 0)) else [],
                    quiet=not last)
        b.op("v", lambda e, c=c, ip=ip: e.tensor_tensor(
            out=xres[:, c * 1024:(c + 1) * 1024],
            in0=xres[:, c * 1024:(c + 1) * 1024],
            in1=pA[:, ip * 1024:(ip + 1) * 1024], op=ALU.add),
            reads=[f"pA{ip}"], writes=[XR[c]])

    if stage == 5:
        emit_debug([dbg_actA(), dbg_xres()])
        return nc, b

    emit_ln(gate=True)
    if stage == 6:
        emit_debug([dbg_actA(), dbg_mask(), dbg_xres()])
        return nc, b
    emit_ffn(final=True)

    fin = ("SO", b.cnt["SO"])
    b.op("y", lambda e: e.nop(), extra=[fin], dma="dC")
    return nc, b


def finish_build(nc, b):
    hs = {}
    for name in ["dC", "dF0", "dF1", "dQ", "dM", "dR", "dK0", "dK1", "dK2",
                 "dV0", "dV1", "dS0", "dS1", "dS2", "dS3", "dH0", "dH1",
                 "CC", "PE", "VE", "SE", "PO", "SO"]:
        hs[name] = nc.alloc_semaphore(name)
    block = nc.Block()
    block.__enter__()
    b.emit(block, hs)
    block.__exit__(None, None, None)
    return nc


def prep_shared(i):
    f32 = np.float32
    bf = ml_dtypes.bfloat16
    c = np.ascontiguousarray
    o = {}
    def img(W, nco, nki):
        return c(np.asarray(W).reshape(nco, 128, nki, 128)
                 .transpose(0, 3, 2, 1).reshape(nco, 128, nki * 128))
    o["wdq"] = img(i["L0_attn_W_dq"], 12, 16)
    o["wuq"] = img(i["L0_attn_W_uq"], 16, 12)
    o["wdkv"] = img(i["L0_attn_W_dkv"], 4, 16)
    o["wuk"] = img(i["L0_attn_W_uk"], 16, 4)
    o["wuv"] = c(np.asarray(i["L0_attn_W_uv"]).reshape(4, 512, 4, 128)
                 .transpose(0, 3, 2, 1).reshape(4, 128, 2048))
    o["wo"] = img(i["L0_attn_W_o"], 16, 16)
    o["expw"] = c(np.asarray(i["L0_ffn_expert_w"])
                  .reshape(E, 16, 128, 16, 128)
                  .transpose(0, 3, 2, 1, 4).reshape(E, 16, 128, 2048))
    Wqr = np.asarray(i["L0_attn_W_qr"])
    o["wqr"] = c(Wqr.T.reshape(12, 128, 64).transpose(1, 0, 2)
                 .reshape(128, 768))
    o["wqrs"] = c(Wqr.reshape(32, 2, DCQ)[:, ::-1, :].reshape(64, DCQ)
                  .T.reshape(12, 128, 64).transpose(1, 0, 2).reshape(128, 768))
    Wkr = np.asarray(i["L0_attn_W_kr"])
    o["wkr"] = c(Wkr.T.reshape(16, 128, 64).transpose(1, 0, 2)
                 .reshape(128, 1024))
    o["wkrs"] = c(Wkr.reshape(32, 2, D)[:, ::-1, :].reshape(64, D)
                  .T.reshape(16, 128, 64).transpose(1, 0, 2).reshape(128, 1024))
    o["gwH"] = c(np.asarray(i["L0_ffn_gate_w"]).reshape(16, 128, 8)
                 .transpose(1, 0, 2).reshape(128, 128))
    o["gbH"] = c(np.asarray(i["L0_ffn_gate_b"]).reshape(8, 1))
    o["ebH"] = c(np.asarray(i["L0_ffn_expert_b"]).reshape(E, 16, 128)
                 .transpose(2, 0, 1).reshape(128, 128))
    o["qnH"] = c(np.asarray(i["L0_attn_q_norm_w"]).reshape(12, 128).T)
    o["kvnH"] = c(np.asarray(i["L0_attn_kv_norm_w"]).reshape(4, 128).T)
    o["identH"] = np.eye(128, dtype=f32)
    o["onesfH"] = np.ones((128, 128), f32)
    o["onesbH"] = np.ones((128, 128), bf)
    fc = np.zeros((128, 12), f32)
    fc[:, 0] = math.log(INV)
    fc[:, 1] = 1e30
    fc[:, 2] = 0.25
    fc[:, 3] = 1e-5
    fc[:, 4] = float(np.finfo(np.float32).eps)
    fc[:, 5] = 1.0 / 2048.0
    fc[:, 6] = 1.0 / 1536.0
    fc[:, 7] = 1.0 / 512.0
    fc[:, 8] = -1.0
    fc[:, 9] = SHIFT
    o["fconsH"] = fc
    inv = 1.0 / (10000.0 ** (np.arange(0, 64, 2, dtype=f32) / 64.0))
    t = np.arange(T, dtype=f32)
    fr = np.outer(inv, t)
    Ct = np.zeros((64, T), f32)
    St = np.zeros((64, T), f32)
    Ct[0::2] = np.cos(fr)
    Ct[1::2] = np.cos(fr)
    St[0::2] = -np.sin(fr)
    St[1::2] = np.sin(fr)
    o["_C"] = Ct.astype(bf)
    o["_S"] = St.astype(bf)
    return o


def make_in_maps(inputs):
    sh = prep_shared(inputs)
    Cf = sh.pop("_C")
    Sf = sh.pop("_S")
    x = np.asarray(inputs["L0_attn_x"])[0]
    xTfull = np.ascontiguousarray(x.T)
    maps = []
    for r in range(NC):
        m = dict(sh)
        m["xT"] = np.ascontiguousarray(xTfull[:, r * TL:(r + 1) * TL])
        m["ropeCH"] = np.ascontiguousarray(Cf[:, r * TL:(r + 1) * TL])
        m["ropeSH"] = np.ascontiguousarray(Sf[:, r * TL:(r + 1) * TL])
        maps.append(m)
    return maps


def run(inputs, stage=7, sim=False):
    nc, b = build_program(stage)
    nc = finish_build(nc, b)
    maps = make_in_maps(inputs)
    names = ["out"]
    if stage == 8:
        names = ["dbg_k0", "dbg_k1", "dbg_qt", "dbg_pt", "dbg_bias", "dbg_vs",
                 "dbg_ao", "dbg_ao1", "dbg_k2"]
    elif stage < 7:
        names = ["dbg_actA", "dbg_mask", "dbg_xres"]
        if stage >= 4:
            names += ["dbg_cq", "dbg_stg"]
        if stage in (3, 4, 5):
            names.remove("dbg_mask")
    if sim:
        from concourse import bass_interp
        s = bass_interp.MultiCoreSim(nc, NC)
        for r in range(NC):
            for k, v in maps[r].items():
                s.cores[r].tensor(k)[:] = v
        s.simulate()
        return [{n: np.asarray(s.cores[r].mem_tensor(n)) for n in names}
                for r in range(NC)]
    res = run_bass_kernel_spmd(nc, maps, core_ids=list(range(NC)))
    return res.results


def kernel(**inputs):
    res = run(inputs, stage=7, sim=False)
    parts = [np.asarray(r["out"]).T for r in res]
    return np.concatenate(parts, axis=0)[None, :, :].astype(np.float32)


# version 11
# speedup vs baseline: 1.3166x; 1.3166x over previous
import math
import numpy as np
import ml_dtypes
import concourse.bass as bass
import concourse.mybir as mybir
from concourse.ap import AP
from concourse.bass_utils import run_bass_kernel_spmd

F32 = mybir.dt.float32
BF16 = mybir.dt.bfloat16
AF = mybir.ActivationFunctionType
ALU = mybir.AluOpType
AXX = mybir.AxisListType.X

NC = 8
T = 8192
TL = T // NC
D = 2048
NH = 32
DCQ = 1536
DC = 512
E = 8
INV = 1.0 / math.sqrt(128.0)
SHIFT = 40.0

R_KC = 0
R_KR = 2048
R_M = 2112
R_V = 2115
NROWS = 4163
N1 = NROWS * 1024
CC1_ROWS = 2115

ENG_SEM = {"t": "PE", "v": "VE", "s": "SE", "g": "PO"}


class Builder:
    def __init__(self):
        self.streams = {e: [] for e in ("t", "v", "s", "g", "y")}
        self.cnt = {}
        self.res = {}

    def _inc(self, name, by):
        self.cnt[name] = self.cnt.get(name, 0) + by
        return (name, self.cnt[name])

    def op(self, eng, fn, reads=(), writes=(), extra=(), dma="dW", sem=None,
           quiet=False):
        own = ENG_SEM.get(eng)
        waits = [w for w in extra if w]
        for r in reads:
            e_ = self.res.get(r)
            if e_ and e_[0]:
                waits.append(e_[0])
        for r in writes:
            e_ = self.res.get(r)
            if e_:
                if e_[0]:
                    waits.append(e_[0])
                waits.extend(e_[1])
        if eng == "t":
            waits = [w for w in waits if w[0] != "PE"]
        if quiet:
            self.streams[eng].append((fn, tuple(waits), None, 0))
            return None
        if sem is None:
            sem = dma if eng == "y" else own
        by = 16 if eng == "y" else 1
        tok = self._inc(sem, by)
        self.streams[eng].append((fn, tuple(waits), sem, by))
        for r in reads:
            self.res.setdefault(r, [None, []])[1].append(tok)
        for r in writes:
            self.res[r] = [tok, []]
        return tok

    def groupify(self, sem):
        tot = (sem, self.cnt.get(sem, 0))
        for r, e_ in self.res.items():
            if e_[0] and e_[0][0] == sem:
                e_[0] = tot
            e_[1] = [tot if t[0] == sem else t for t in e_[1]]

    def emit(self, block, hs):
        def mk(ename):
            waited = {}

            def _run(eng):
                for fn, waits, sem, by in self.streams[ename]:
                    for (s, v) in waits:
                        if waited.get(s, 0) < v:
                            eng.wait_ge(hs[s], v)
                            waited[s] = v
                    ins = fn(eng)
                    if sem is not None:
                        ins.then_inc(hs[sem], by)
            return _run
        block.tensor(mk("t"))
        block.vector(mk("v"))
        block.scalar(mk("s"))
        block.gpsimd(mk("g"))
        block.sync(mk("y"))


def build_program(stage=7):
    nc = bass.Bass("TRN2", target_bir_lowering=False)
    b = Builder()

    def din(name, shape, dt=F32):
        return nc.dram_tensor(name, shape, dt, kind="ExternalInput")

    xT = din("xT", [D, TL])
    wdq = din("wdq", [12, 128, 2048])
    wuq = din("wuq", [16, 128, DCQ])
    wdkv = din("wdkv", [4, 128, 2048])
    wuk = din("wuk", [16, 128, DC])
    wuv = din("wuv", [4, 128, 2048])
    wo = din("wo", [16, 128, 2048])
    expw = din("expw", [E, 16, 128, 2048])
    wqr = din("wqr", [128, 768])
    wqrs = din("wqrs", [128, 768])
    wkr = din("wkr", [128, 1024])
    wkrs = din("wkrs", [128, 1024])
    gwH = din("gwH", [128, 128])
    gbH = din("gbH", [8, 1])
    ebH = din("ebH", [128, 128])
    qnH = din("qnH", [128, 12])
    kvnH = din("kvnH", [128, 4])
    identH = din("identH", [128, 128])
    onesfH = din("onesfH", [128, 128])
    onesbH = din("onesbH", [128, 128], BF16)
    fconsH = din("fconsH", [128, 12])
    ropeCH = din("ropeCH", [64, TL], BF16)
    ropeSH = din("ropeSH", [64, TL], BF16)

    out = nc.dram_tensor("out", [D, TL], F32, kind="ExternalOutput")
    dbg = {}
    if stage < 7 or stage == 8:
        dbg["actA"] = nc.dram_tensor("dbg_actA", [128, 16 * 1024], BF16,
                                     kind="ExternalOutput")
        dbg["mask"] = nc.dram_tensor("dbg_mask", [8, 1024], F32,
                                     kind="ExternalOutput")
        dbg["xres"] = nc.dram_tensor("dbg_xres", [128, 16 * 1024], F32,
                                     kind="ExternalOutput")
        if stage >= 4:
            dbg["cq"] = nc.dram_tensor("dbg_cq", [128, 12 * 1024], BF16,
                                       kind="ExternalOutput")
            dbg["stg"] = nc.dram_tensor("dbg_stg", [NROWS, 1024], BF16,
                                        kind="ExternalOutput")
        if stage == 8:
            dbg["k0"] = nc.dram_tensor("dbg_k0", [128, 4096], BF16,
                                       kind="ExternalOutput")
            dbg["k1"] = nc.dram_tensor("dbg_k1", [128, 4096], BF16,
                                       kind="ExternalOutput")
            dbg["qt"] = nc.dram_tensor("dbg_qt", [128, 4096], BF16,
                                       kind="ExternalOutput")
            dbg["pt"] = nc.dram_tensor("dbg_pt", [128, 2048], BF16,
                                       kind="ExternalOutput")
            dbg["bias"] = nc.dram_tensor("dbg_bias", [128, 4], F32,
                                         kind="ExternalOutput")
            dbg["vs"] = nc.dram_tensor("dbg_vs", [128, 4160], BF16,
                                       kind="ExternalOutput")
            dbg["ao"] = nc.dram_tensor("dbg_ao", [64, 1024], BF16,
                                       kind="ExternalOutput")
            dbg["ao1"] = nc.dram_tensor("dbg_ao1", [64, 1024], BF16,
                                        kind="ExternalOutput")
            dbg["k2"] = nc.dram_tensor("dbg_k2", [128, 4096], BF16,
                                       kind="ExternalOutput")

    stage_in = nc.dram_tensor("stage_in", [N1], BF16)
    C1N = CC1_ROWS * 1024
    C2N = (NROWS - CC1_ROWS) * 1024
    stage_out = nc.dram_tensor("stage_out", [NC * C1N], BF16, addr_space="Shared")
    stage_out2 = nc.dram_tensor("stage_out2", [NC * C2N], BF16, addr_space="Shared")

    def sba(name, shape, dt):
        return nc.alloc_sbuf_tensor(name, shape, dt).ap()

    xres = sba("xres", [128, 16 * 1024], F32)
    actA = sba("actA", [128, 16 * 1024], BF16)
    cq = sba("cq", [128, 12 * 1024], BF16)
    shAt = nc.alloc_sbuf_tensor("shA", [128, 24704], BF16)
    shA = shAt.ap()
    shAf = shA.bitcast(F32)
    qt = sba("qt", [128, 4 * 1024], BF16)
    pt = sba("pt", [128, 2 * 1024], BF16)
    qsq = sba("qsq", [128, 1024], BF16)
    wbf = sba("wbf", [128, 2 * 2048], BF16)
    onesb = sba("onesb", [128, 128], BF16)
    mglob = sba("mglob", [128, 24], BF16)
    msq = sba("msq", [128, 24], BF16)
    mloc = sba("mloc", [128, 24], BF16)
    mgat = sba("mgat", [128, 192], BF16)
    wf32 = sba("wf32", [128, 1024], F32)
    scr1 = sba("scr1", [128, 1024], F32)
    scr2 = sba("scr2", [128, 1024], F32)
    ebT = sba("ebT", [128, 128], F32)
    gwT = sba("gwT", [128, 128], F32)
    gbT = sba("gbT", [8, 1], F32)
    qnT = sba("qnT", [128, 12], F32)
    kvnT = sba("kvnT", [128, 4], F32)
    identT = sba("identT", [128, 128], F32)
    onesf = sba("onesf", [128, 128], F32)
    fcons = sba("fcons", [128, 12], F32)
    sc = sba("sc", [128, 16], F32)
    biasb = sba("biasb", [128, 4], F32)
    gtm = sba("gtm", [128, 64], F32)
    selm = sba("selm", [128, 64], F32)
    gscr = sba("gscr", [128, 64], F32)
    fscr = sba("fscr", [128, 24], F32)

    def bfv(off, plen, cols):
        return shA[0:plen, off:off + cols]

    def f32v(off, plen, cols):
        return shAf[0:plen, off // 2:off // 2 + cols]

    def U(off, cols):
        return [f"u{k}" for k in range(off // 1024, (off + cols + 1023) // 1024)]

    rowm = lambda: f32v(0, 1, 1024)
    K_rowm = U(0, 2048)
    rowsq = lambda: f32v(2048, 1, 1024)
    K_rowsq = U(2048, 2048)
    rowt = lambda: f32v(4096, 1, 1024)
    K_rowt = U(4096, 2048)
    glrow = lambda: f32v(6144, 8, 1024)
    K_glrow = U(6144, 2048)
    rowe = lambda: f32v(8192, 1, 1024)
    K_rowe = U(8192, 2048)
    maskbv = lambda s: bfv(10240 + s * 1024, 128, 1024)
    K_maskb = lambda s: U(10240 + s * 1024, 1024)
    qslot = lambda s: bfv(12288 + s * 1024, 128, 1024)
    K_qslot = lambda s: U(12288 + s * 1024, 1024)
    nrmT = lambda s: f32v(12288 + s * 2048, 128, 1024)
    NRM_KEYS = [U(12288, 2048), U(14336, 2048)]
    ropeC = lambda: bfv(0, 64, 1024)
    K_ropeC = U(0, 1024)
    ropeS = lambda: bfv(1024, 64, 1024)
    K_ropeS = U(1024, 1024)
    kropev = lambda: bfv(2048, 64, 1024)
    K_krope = U(2048, 1024)
    kconv = lambda s: bfv(3072 + s * 1024, 128, 1024)
    K_kcon = lambda s: U(3072 + s * 1024, 1024)
    qropev = lambda: bfv(5120, 64, 1024)
    K_qrope = U(5120, 1024)
    wqrb = lambda: bfv(6144, 128, 768)
    K_wqrb = U(6144, 768)
    wqrsb = lambda: bfv(6912, 128, 768)
    K_wqrsb = U(6912, 768)
    wkrb = lambda: bfv(7680, 128, 1024)
    K_wkrb = U(7680, 1024)
    wkrsb = lambda: bfv(8704, 128, 1024)
    K_wkrsb = U(8704, 1024)
    ckvv = lambda: bfv(9728, 128, 4096)
    K_ckv = U(9728, 4096)
    rowP = lambda: f32v(13824, 1, 1024)
    ROWP_KEYS = U(13824, 2048)
    tmphv = lambda s: bfv(14336 + s * 1024, 64, 1024)
    K_tmph = lambda s: U(14336 + s * 1024, 1024)
    kring = lambda s: bfv(s * 4096, 128, 4096)
    wbig = lambda i: f32v(16384 + i * 4160, 128, 2048)
    K_kring = lambda s: U(s * 4096, 4096)
    vslot = lambda s: bfv(16384 + s * 4160, 128, 4160)
    K_vslot = lambda s: [f"vs{s}"]

    pA = nc.alloc_psum_tensor("pA", [128, 2048], F32).ap()
    pB = nc.alloc_psum_tensor("pB", [128, 1024], F32).ap()
    pE = nc.alloc_psum_tensor("pE", [128, 1024], F32).ap()

    XR = [f"xr{c}" for c in range(16)]
    AC = [f"ac{c}" for c in range(16)]

    for (d_, s_) in [(ebT, ebH), (gwT, gwH), (gbT, gbH), (qnT, qnH),
                     (kvnT, kvnH), (identT, identH), (onesf, onesfH),
                     (onesb, onesbH), (fcons, fconsH)]:
        b.op("y", lambda e, d=d_, s=s_: e.dma_start(out=d, in_=s.ap()),
             writes=["K"], dma="dC")
    b.op("y", lambda e: e.dma_start(
        out=xres.rearrange("p (c t) -> p c t", t=1024),
        in_=xT.ap().rearrange("(c p) t -> p c t", p=128)),
        writes=XR, dma="dC")
    b.groupify("dC")
    b.op("g", lambda e: e.memset(mloc[:, :], 0.0), writes=["mloc"])

    _wb = {"i": 0}

    def wstream_big(srcap, ncols, slot_ap, slot_key, eng=None):
        assert not isinstance(slot_key, list)
        hp = _wb["i"] % 2
        _wb["i"] += 1
        wk = f"vs{hp}"
        b.op("y", lambda e, src=srcap, n=ncols, hp=hp: e.dma_start(
            out=wbig(hp)[:, 0:n], in_=src), writes=[wk], dma=f"dF{hp}")
        h = ncols // 2
        b.op("g", lambda e, h=h, sl=slot_ap, hp=hp: e.tensor_copy(
            out=sl[:, 0:h], in_=wbig(hp)[:, 0:h]),
            reads=[wk], writes=[slot_key + "a"])
        b.op("v", lambda e, h=h, n=ncols, sl=slot_ap, hp=hp: e.tensor_copy(
            out=sl[:, h:n], in_=wbig(hp)[:, h:n]),
            reads=[wk], writes=[slot_key + "b"])

    def wstream(srcaps, ncols_each, slot_ap, slot_key, eng="g"):
        keys = slot_key if isinstance(slot_key, list) else [slot_key + "a"]
        coff = 0
        for srcap, ncols in zip(srcaps, ncols_each):
            assert ncols <= 512, ncols
            hp = (coff // 512) % 2
            wk = f"wf32{hp}"
            b.op("y", lambda e, s=srcap, n=ncols, hp=hp: e.dma_start(
                out=wf32[:, hp * 512:hp * 512 + n], in_=s),
                writes=[wk], dma=f"dF{hp}")
            b.op(eng, lambda e, n=ncols, co=coff, sl=slot_ap, hp=hp: e.tensor_copy(
                out=sl[:, co:co + n], in_=wf32[:, hp * 512:hp * 512 + n]),
                reads=[wk], writes=keys)
            coff += ncols

    def bcast_cols(row_ap, row_key, dst_ap, dst_key, psum_key="pA1"):
        keys = row_key if isinstance(row_key, list) else [row_key]
        for qb in range(2):
            b.op("t", lambda e, qb=qb: e.matmul(
                pA[:, 1024 + qb * 512:1024 + (qb + 1) * 512],
                onesf[0:1, 0:128], row_ap[0:1, qb * 512:(qb + 1) * 512],
                start=True, stop=True), reads=keys + ["K"], writes=[psum_key])
        b.op("v", lambda e: e.tensor_copy(out=dst_ap, in_=pA[:, 1024:2048]),
             reads=[psum_key], writes=[dst_key])

    def emit_ln(gate):
        for c in range(16):
            b.op("v", lambda e, c=c: e.tensor_copy(
                out=actA[:, c * 1024:(c + 1) * 1024],
                in_=xres[:, c * 1024:(c + 1) * 1024]),
                reads=[XR[c], "K"], writes=[AC[c]])
        for c in range(16):
            for qb in range(2):
                b.op("t", lambda e, c=c, qb=qb: e.matmul(
                    pE[0:1, qb * 512:(qb + 1) * 512], onesb[:, 0:1],
                    actA[:, c * 1024 + qb * 512:c * 1024 + (qb + 1) * 512],
                    start=(c == 0), stop=(c == 15)),
                    reads=[AC[c], "K"], writes=["pE"])
            b.op("v", lambda e, c=c: e.tensor_tensor(
                out=qslot(c % 2), in0=actA[:, c * 1024:(c + 1) * 1024],
                in1=actA[:, c * 1024:(c + 1) * 1024], op=ALU.mult),
                reads=[AC[c]], writes=K_qslot(c % 2))
            for qb in range(2):
                b.op("t", lambda e, c=c, qb=qb: e.matmul(
                    pB[0:1, qb * 512:(qb + 1) * 512], onesb[:, 0:1],
                    qslot(c % 2)[:, qb * 512:(qb + 1) * 512],
                    start=(c == 0), stop=(c == 15)),
                    reads=K_qslot(c % 2) + ["K"], writes=[f"pB{qb}"])
        b.op("v", lambda e: e.tensor_scalar(
            out=rowm(), in0=pE[0:1, 0:1024], scalar1=fcons[0:1, 5:6],
            scalar2=None, op0=ALU.mult), reads=["pE", "K"], writes=K_rowm)
        b.op("v", lambda e: e.tensor_scalar(
            out=rowsq(), in0=pB[0:1, 0:1024], scalar1=fcons[0:1, 5:6],
            scalar2=None, op0=ALU.mult), reads=["pB0", "pB1", "K"],
            writes=K_rowsq)
        b.op("v", lambda e: e.tensor_tensor(
            out=rowt(), in0=rowm(), in1=rowm(), op=ALU.mult),
            reads=K_rowm, writes=K_rowt)
        b.op("v", lambda e: e.tensor_tensor(
            out=rowsq(), in0=rowsq(), in1=rowt(), op=ALU.subtract),
            reads=K_rowt, writes=K_rowsq)
        b.op("s", lambda e: e.activation(
            out=rowt(), in_=rowsq(), func=AF.Ln, bias=fcons[0:1, 3:4]),
            reads=K_rowsq + ["K"], writes=K_rowt)
        b.op("s", lambda e: e.activation(
            out=rowsq(), in_=rowt(), func=AF.Exp, scale=-0.5),
            reads=["rowt"], writes=K_rowsq)
        bcast_cols(rowm(), K_rowm, scr1[:, :], "scr1")
        bcast_cols(rowsq(), K_rowsq, scr2[:, :], "scr2")
        for c in range(16):
            nk = NRM_KEYS[c % 2]
            b.op("v", lambda e, c=c: e.tensor_tensor(
                out=nrmT(c % 2), in0=xres[:, c * 1024:(c + 1) * 1024],
                in1=scr1[:, :], op=ALU.subtract),
                reads=[XR[c], "scr1"], writes=nk)
            if gate:
                b.op("v", lambda e, c=c: e.tensor_tensor(
                    out=nrmT(c % 2), in0=nrmT(c % 2), in1=scr2[:, :],
                    op=ALU.mult), reads=["scr2"], writes=nk)
                for qb in range(2):
                    b.op("t", lambda e, c=c, qb=qb: e.matmul(
                        pA[0:8, qb * 512:(qb + 1) * 512],
                        gwT[:, c * 8:(c + 1) * 8],
                        nrmT(c % 2)[:, qb * 512:(qb + 1) * 512],
                        start=(c == 0), stop=(c == 15)),
                        reads=nk + ["K"], writes=["pA0"])
                b.op("v", lambda e, c=c: e.tensor_copy(
                    out=actA[:, c * 1024:(c + 1) * 1024], in_=nrmT(c % 2)),
                    reads=nk, writes=[AC[c]])
            else:
                b.op("v", lambda e, c=c: e.tensor_tensor(
                    out=actA[:, c * 1024:(c + 1) * 1024], in0=nrmT(c % 2),
                    in1=scr2[:, :], op=ALU.mult),
                    reads=nk + ["scr2"], writes=[AC[c]])
        if not gate:
            return
        b.op("s", lambda e: e.activation(
            out=glrow(), in_=pA[0:8, 0:1024], func=AF.Identity,
            bias=gbT[0:8, 0:1]), reads=["pA0", "K"], writes=K_glrow)
        for i in range(8):
            b.op("t", lambda e, i=i: e.transpose(
                pB[0:128, i * 8:(i + 1) * 8],
                glrow()[0:8, i * 128:(i + 1) * 128], identT[0:8, 0:8]),
                reads=K_glrow + ["K"], writes=["pB0"])
        b.op("v", lambda e: e.tensor_copy(out=gtm[:, :], in_=pB[:, 0:64]),
             reads=["pB0"], writes=["gtm"])
        b.op("v", lambda e: e.reduce_max(
            out=fscr[:, 0:8], in_=gtm[:, :].rearrange("p (a b) -> p a b", b=8),
            axis=AXX), reads=["gtm"], writes=["fscr1"])
        for i in range(8):
            b.op("v", lambda e, i=i: e.tensor_scalar(
                out=selm[:, i * 8:(i + 1) * 8], in0=gtm[:, i * 8:(i + 1) * 8],
                scalar1=fscr[:, i:i + 1], scalar2=None, op0=ALU.is_equal),
                reads=["gtm", "fscr1"], writes=["selm"])
        b.op("v", lambda e: e.tensor_scalar(
            out=gscr[:, :], in0=selm[:, :], scalar1=fcons[0:128, 1:2],
            scalar2=None, op0=ALU.mult), reads=["selm", "K"], writes=["gscr"])
        b.op("v", lambda e: e.tensor_tensor(
            out=gscr[:, :], in0=gtm[:, :], in1=gscr[:, :], op=ALU.subtract),
            reads=["gtm"], writes=["gscr"])
        b.op("v", lambda e: e.reduce_max(
            out=fscr[:, 8:16], in_=gscr[:, :].rearrange("p (a b) -> p a b", b=8),
            axis=AXX), reads=["gscr"], writes=["fscr2"])
        for i in range(8):
            b.op("v", lambda e, i=i: e.tensor_scalar(
                out=selm[:, i * 8:(i + 1) * 8], in0=gtm[:, i * 8:(i + 1) * 8],
                scalar1=fscr[:, 8 + i:9 + i], op0=ALU.is_ge,
                scalar2=fcons[0:128, 2:3], op1=ALU.mult),
                reads=["gtm", "fscr2", "K"], writes=["selm"])
        for i in range(8):
            b.op("t", lambda e, i=i: e.transpose(
                pB[0:8, i * 128:(i + 1) * 128], selm[:, i * 8:(i + 1) * 8],
                identT[:, :]), reads=["selm", "K"],
                writes=["pB0" if i < 4 else "pB1"])
        b.op("v", lambda e: e.tensor_copy(
            out=scr2[0:8, :], in_=pB[0:8, 0:1024]),
            reads=["pB0", "pB1"], writes=["scr2"])

    def emit_ffn(final=False):
        def ffn_stream(idx):
            e_, c = idx // 16, idx % 16
            ip = idx % 2
            wstream_big(expw.ap()[e_, c, :, :],
                        2048, wbf[:, ip * 2048:(ip + 1) * 2048], f"wb{ip}")
        for e_ in range(E):
            sl = e_ % 2
            b.op("y", lambda e, e_=e_: e.dma_start(
                out=rowe(), in_=scr2[e_:e_ + 1, :]),
                reads=["scr2"], writes=K_rowe, dma="dR")
            for qb in range(2):
                b.op("t", lambda e, qb=qb: e.matmul(
                    pE[:, qb * 512:(qb + 1) * 512], onesf[0:1, 0:128],
                    rowe()[0:1, qb * 512:(qb + 1) * 512], start=True, stop=True),
                    reads=K_rowe + ["K"], writes=["pE"])
            b.op("v", lambda e, sl=sl: e.tensor_copy(
                out=maskbv(sl), in_=pE[:, 0:1024]),
                reads=["pE"], writes=K_maskb(sl))
            for c in range(16):
                idx = e_ * 16 + c
                ip = idx % 2
                if idx == 0:
                    ffn_stream(0)
                if idx + 1 < E * 16:
                    ffn_stream(idx + 1)
                for kk in range(16):
                    for qb in range(2):
                        last = (kk == 15 and qb == 1)
                        b.op("t", lambda e, kk=kk, qb=qb, ip=ip: e.matmul(
                            pA[:, ip * 1024 + qb * 512:ip * 1024 + (qb + 1) * 512],
                            wbf[:, ip * 2048 + kk * 128:ip * 2048 + (kk + 1) * 128],
                            actA[:, kk * 1024 + qb * 512:kk * 1024 + (qb + 1) * 512],
                            start=(kk == 0), stop=(kk == 15)),
                            reads=[f"wb{ip}a", f"wb{ip}b"] + (AC if last or kk == 0 else []),
                            writes=[f"pA{ip}"] if (last or (kk == 0 and qb == 0)) else [],
                            quiet=not last)
                b.op("s", lambda e, idx=idx, ip=ip: e.activation(
                    out=qslot(ip), in_=pA[:, ip * 1024:(ip + 1) * 1024],
                    func=AF.Relu, bias=ebT[:, idx % 128:idx % 128 + 1]),
                    reads=[f"pA{ip}", "K"], writes=K_qslot(ip))
                b.op("v", lambda e, ip=ip, sl=sl: e.tensor_tensor(
                    out=pt[:, ip * 1024:(ip + 1) * 1024], in0=qslot(ip),
                    in1=maskbv(sl), op=ALU.mult),
                    reads=K_qslot(ip) + K_maskb(sl), writes=[f"pt{ip}"])
                b.op("v", lambda e, c=c, ip=ip: e.tensor_tensor(
                    out=xres[:, c * 1024:(c + 1) * 1024],
                    in0=xres[:, c * 1024:(c + 1) * 1024],
                    in1=pt[:, ip * 1024:(ip + 1) * 1024], op=ALU.add),
                    reads=[f"pt{ip}"], writes=[XR[c]])
        if final:
            for c in range(16):
                b.op("y", lambda e, c=c: e.dma_start(
                    out=out.ap()[c * 128:(c + 1) * 128, :],
                    in_=xres[:, c * 1024:(c + 1) * 1024]),
                    reads=[XR[c]], dma="SO")

    def emit_debug(items):
        for fn, reads in items:
            b.op("y", lambda e, fn=fn: fn(e), reads=reads, dma="SO")
        fin = ("SO", b.cnt.get("SO", 0))
        b.op("y", lambda e: e.nop(), extra=[fin], dma="dC")

    dbg_actA = lambda: (lambda e: e.dma_start(out=dbg["actA"].ap(), in_=actA), AC)
    dbg_mask = lambda: (lambda e: e.dma_start(out=dbg["mask"].ap(), in_=scr2[0:8, :]), ["scr2"])
    dbg_xres = lambda: (lambda e: e.dma_start(out=dbg["xres"].ap(), in_=xres), XR)

    emit_ln(gate=True)
    if stage == 1:
        emit_debug([dbg_actA(), dbg_mask(), dbg_xres()])
        return nc, b
    emit_ffn()
    if stage == 2:
        emit_debug([dbg_actA(), dbg_mask(), dbg_xres()])
        return nc, b
    emit_ln(gate=False)
    if stage == 3:
        emit_debug([dbg_actA(), dbg_xres()])
        return nc, b

    b.op("y", lambda e: e.dma_start(out=ropeC(), in_=ropeCH.ap()),
         writes=K_ropeC, dma="dQ")
    b.op("y", lambda e: e.dma_start(out=ropeS(), in_=ropeSH.ap()),
         writes=K_ropeS, dma="dQ")
    b.groupify("dQ")
    wstream([wqr.ap()[:, 0:512], wqr.ap()[:, 512:768]],
            [512, 256], wqrb(), K_wqrb)
    wstream([wqrs.ap()[:, 0:512], wqrs.ap()[:, 512:768]],
            [512, 256], wqrsb(), K_wqrsb)
    wstream([wkr.ap()[:, 0:512], wkr.ap()[:, 512:1024]],
            [512, 512], wkrb(), K_wkrb)
    wstream([wkrs.ap()[:, 0:512], wkrs.ap()[:, 512:1024]],
            [512, 512], wkrsb(), K_wkrsb)
    b.res.setdefault("glrow", [None, []])
    b.res.setdefault("rowe", [None, []])

    def rmsnorm_block(nchunks, wsrc, dest_fn, dest_key, gamma_col_fn, scol,
                      dest_units=None, weng="g"):
        def nrm_stream(c):
            ip = c % 2
            wstream_big(wsrc.ap()[c, :, :],
                2048, wbf[:, ip * 2048:(ip + 1) * 2048], f"wb{ip}")
        for c in range(nchunks):
            ip = c % 2
            if c == 0:
                nrm_stream(0)
            if c + 1 < nchunks:
                nrm_stream(c + 1)
            for kk in range(16):
                for qb in range(2):
                    last = (kk == 15 and qb == 1)
                    b.op("t", lambda e, kk=kk, qb=qb, ip=ip: e.matmul(
                        pA[:, ip * 1024 + qb * 512:ip * 1024 + (qb + 1) * 512],
                        wbf[:, ip * 2048 + kk * 128:ip * 2048 + (kk + 1) * 128],
                        actA[:, kk * 1024 + qb * 512:kk * 1024 + (qb + 1) * 512],
                        start=(kk == 0), stop=(kk == 15)),
                        reads=[f"wb{ip}a", f"wb{ip}b"] + (AC if last or kk == 0 else []),
                        writes=[f"pA{ip}"] if (last or (kk == 0 and qb == 0)) else [],
                        quiet=not last)
            du = dest_units(c) if dest_units else []
            b.op("v", lambda e, c=c, ip=ip: e.tensor_copy(
                out=dest_fn(c), in_=pA[:, ip * 1024:(ip + 1) * 1024]),
                reads=[f"pA{ip}"], writes=[f"{dest_key}{c}"] + du)
            b.op("v", lambda e, c=c, ip=ip: e.tensor_tensor(
                out=pt[:, ip * 1024:(ip + 1) * 1024], in0=dest_fn(c),
                in1=dest_fn(c), op=ALU.mult),
                reads=[f"{dest_key}{c}"], writes=[f"pt{ip}"])
            for qb in range(2):
                b.op("t", lambda e, c=c, qb=qb, ip=ip, n=nchunks: e.matmul(
                    pE[0:1, qb * 512:(qb + 1) * 512], onesb[:, 0:1],
                    pt[:, ip * 1024 + qb * 512:ip * 1024 + (qb + 1) * 512],
                    start=(c == 0), stop=(c == n - 1)),
                    reads=[f"pt{ip}", "K"], writes=["pE"])
        b.op("v", lambda e, scol=scol: e.tensor_scalar(
            out=rowP(), in0=pE[0:1, 0:1024], scalar1=fcons[0:1, scol:scol + 1],
            scalar2=None, op0=ALU.mult), reads=["pE", "K"], writes=ROWP_KEYS)
        b.op("s", lambda e: e.activation(
            out=scr1[0:1, :], in_=rowP(), func=AF.Ln, bias=fcons[0:1, 4:5]),
            reads=ROWP_KEYS + ["K"], writes=["scr1"])
        b.op("s", lambda e: e.activation(
            out=rowP(), in_=scr1[0:1, :], func=AF.Exp, scale=-0.5),
            reads=["scr1"], writes=ROWP_KEYS)
        for qb in range(2):
            b.op("t", lambda e, qb=qb: e.matmul(
                pA[:, 1024 + qb * 512:1024 + (qb + 1) * 512],
                onesf[0:1, 0:128], rowP()[0:1, qb * 512:(qb + 1) * 512],
                start=True, stop=True), reads=ROWP_KEYS + ["K"], writes=["pA1"])
        b.op("v", lambda e: e.tensor_copy(out=scr2[:, :], in_=pA[:, 1024:2048]),
             reads=["pA1"], writes=["scr2"])
        for c in range(nchunks):
            du = dest_units(c) if dest_units else []
            b.op("v", lambda e, c=c: e.tensor_tensor(
                out=dest_fn(c), in0=dest_fn(c), in1=scr2[:, :], op=ALU.mult),
                reads=["scr2"], writes=[f"{dest_key}{c}"] + du)
            b.op("v", lambda e, c=c: e.tensor_scalar(
                out=dest_fn(c), in0=dest_fn(c), scalar1=gamma_col_fn(c),
                scalar2=None, op0=ALU.mult), reads=["K"],
                writes=[f"{dest_key}{c}"] + du)

    def rope_build(wbk, wsbk, wb_ap, wsb_ap, nkk, rhs_fn, rhs_keys, dest_ap,
                   dest_keys):
        for kk in range(nkk):
            for qb in range(2):
                b.op("t", lambda e, kk=kk, qb=qb: e.matmul(
                    pA[0:64, qb * 512:(qb + 1) * 512],
                    wb_ap[:, kk * 64:(kk + 1) * 64],
                    rhs_fn(kk)[:, qb * 512:(qb + 1) * 512],
                    start=(kk == 0), stop=(kk == nkk - 1)),
                    reads=wbk + [rhs_keys[kk]], writes=["pA0"])
        for kk in range(nkk):
            for qb in range(2):
                b.op("t", lambda e, kk=kk, qb=qb: e.matmul(
                    pA[0:64, 1024 + qb * 512:1024 + (qb + 1) * 512],
                    wsb_ap[:, kk * 64:(kk + 1) * 64],
                    rhs_fn(kk)[:, qb * 512:(qb + 1) * 512],
                    start=(kk == 0), stop=(kk == nkk - 1)),
                    reads=wsbk + [rhs_keys[kk]], writes=["pA1"])
        b.op("v", lambda e: e.tensor_tensor(
            out=scr1[0:64, :], in0=pA[0:64, 0:1024], in1=ropeC(), op=ALU.mult),
            reads=["pA0"] + K_ropeC, writes=["scr1"])
        b.op("v", lambda e: e.tensor_tensor(
            out=scr2[0:64, :], in0=pA[0:64, 1024:2048], in1=ropeS(),
            op=ALU.mult), reads=["pA1"] + K_ropeS, writes=["scr2"])
        b.op("v", lambda e, d=dest_ap: e.tensor_tensor(
            out=d, in0=scr1[0:64, :], in1=scr2[0:64, :], op=ALU.add),
            reads=["scr1", "scr2"], writes=dest_keys)

    rmsnorm_block(4, wdkv, lambda c: ckvv()[:, c * 1024:(c + 1) * 1024], "ck",
                  lambda c: kvnT[:, c:c + 1], 7,
                  dest_units=lambda c: U(9728 + c * 1024, 1024))

    def kc_stream(c):
        ip = c % 2
        wstream_big(wuk.ap()[c, :, :],
                    512, wbf[:, ip * 2048:ip * 2048 + 512], f"wb{ip}")
    for c in range(16):
        ip = c % 2
        if c == 0:
            kc_stream(0)
        if c + 1 < 16:
            kc_stream(c + 1)
        for kk in range(4):
            for qb in range(2):
                b.op("t", lambda e, kk=kk, qb=qb, ip=ip: e.matmul(
                    pA[:, ip * 1024 + qb * 512:ip * 1024 + (qb + 1) * 512],
                    wbf[:, ip * 2048 + kk * 128:ip * 2048 + (kk + 1) * 128],
                    ckvv()[:, kk * 1024 + qb * 512:kk * 1024 + (qb + 1) * 512],
                    start=(kk == 0), stop=(kk == 3)),
                    reads=[f"wb{ip}a", f"wb{ip}b", f"ck{kk}"] + U(9728 + kk * 1024, 1024),
                    writes=[f"pA{ip}"])
        b.op("v", lambda e, ip=ip: e.tensor_copy(
            out=kconv(ip), in_=pA[:, ip * 1024:(ip + 1) * 1024]),
            reads=[f"pA{ip}"], writes=K_kcon(ip))
        b.op("v", lambda e, c=c, ip=ip: e.reduce_max(
            out=mloc[:, c:c + 1], in_=kconv(ip), axis=AXX,
            apply_absolute_value=True), reads=K_kcon(ip), writes=["mloc"])
        b.op("y", lambda e, c=c, ip=ip: e.dma_start(
            out=AP(tensor=stage_in, offset=(R_KC + c * 128) * 1024,
                   ap=[[1024, 128], [1, 1024]]),
            in_=kconv(ip)), reads=K_kcon(ip) + ["stgin1"], dma="dS%d" % ip)

    rope_build(K_wkrb, K_wkrsb, wkrb(), wkrsb(), 16,
               lambda kk: actA[:, kk * 1024:(kk + 1) * 1024],
               AC, kropev(), K_krope)
    b.op("v", lambda e: e.reduce_max(
        out=mloc[0:64, 16:17], in_=kropev(), axis=AXX,
        apply_absolute_value=True), reads=K_krope, writes=["mloc"])
    b.op("y", lambda e: e.dma_start(
        out=AP(tensor=stage_in, offset=R_KR * 1024, ap=[[1024, 64], [1, 1024]]),
        in_=kropev()), reads=K_krope + ["stgin1"], dma="dM")
    b.op("y", lambda e: e.dma_start(
        out=AP(tensor=stage_in, offset=R_M * 1024, ap=[[24, 128], [1, 24]]),
        in_=mloc[:, :]), reads=["mloc", "stgin1"], dma="dM")
    b.groupify("dM")

    cc1 = b.op("g", lambda e: e.collective_compute(
        "AllGather", ALU.bypass, replica_groups=[list(range(NC))],
        ins=[stage_in.ap()[0:CC1_ROWS * 1024]
             .rearrange("(a b) -> a b", b=1024).opt()],
        outs=[stage_out.ap().rearrange("(a b) -> a b", b=1024).opt()]),
        writes=["stgin1", "stgout"], sem="CC")

    for r in range(NC):
        b.op("y", lambda e, r=r: e.dma_start(
            out=mgat[:, r * 24:(r + 1) * 24],
            in_=AP(tensor=stage_out, offset=r * C1N + R_M * 1024,
                   ap=[[24, 128], [1, 24]])),
            reads=["stgout"], writes=["mgat"], dma="dM")
    b.groupify("dM")
    b.op("v", lambda e: e.reduce_max(
        out=mglob[:, :], in_=mgat[:, :].rearrange("p (r c) -> p c r", c=24),
        axis=AXX), reads=["mgat"], writes=["mglob"])
    b.op("v", lambda e: e.tensor_tensor(
        out=msq[:, :], in0=mglob[:, :], in1=mglob[:, :], op=ALU.mult),
        reads=["mglob"], writes=["msq"])

    def vb_stream(ob):
        ip = ob % 2
        wstream_big(wuv.ap()[ob, :, :],
                    2048, wbf[:, ip * 2048:(ip + 1) * 2048], f"wb{ip}")
    for ob in range(4):
        ip = ob % 2
        if ob == 0:
            vb_stream(0)
        if ob + 1 < 4:
            vb_stream(ob + 1)
        for tt in range(8):
            tp = tt % 2
            for dc in range(4):
                b.op("t", lambda e, tt=tt, dc=dc, ip=ip, tp=tp: e.matmul(
                    pB[:, tp * 512:(tp + 1) * 512],
                    ckvv()[:, dc * 1024 + tt * 128:dc * 1024 + tt * 128 + 128],
                    wbf[:, ip * 2048 + dc * 512:ip * 2048 + (dc + 1) * 512],
                    start=(dc == 0), stop=(dc == 3)),
                    reads=[f"wb{ip}a", f"wb{ip}b", f"ck{dc}"] + U(9728 + dc * 1024, 1024),
                    writes=[f"pB{tp}"])
            b.op("v", lambda e, tp=tp: e.tensor_copy(
                out=pt[:, tp * 1024:tp * 1024 + 512],
                in_=pB[:, tp * 512:(tp + 1) * 512]),
                reads=[f"pB{tp}"], writes=[f"pt{tp}"])
            b.op("y", lambda e, tt=tt, ob=ob, tp=tp: e.dma_start(
                out=AP(tensor=stage_in,
                       offset=R_V * 1024 + tt * 128 * 2048 + ob * 512,
                       ap=[[2048, 128], [1, 512]]),
                in_=pt[:, tp * 1024:tp * 1024 + 512]),
                reads=[f"pt{tp}", "stgin2"], dma="dS%d" % (2 + tp))

    cc2 = b.op("g", lambda e: e.collective_compute(
        "AllGather", ALU.bypass, replica_groups=[list(range(NC))],
        ins=[stage_in.ap()[CC1_ROWS * 1024:]
             .rearrange("(a b) -> a b", b=1024).opt()],
        outs=[stage_out2.ap().rearrange("(a b) -> a b", b=1024).opt()]),
        writes=["stgin2", "stgout2"], sem="CC")
    rmsnorm_block(12, wdq, lambda c: cq[:, c * 1024:(c + 1) * 1024], "cq",
                  lambda c: qnT[:, c:c + 1], 6, weng="v")
    rope_build(K_wqrb, K_wqrsb, wqrb(), wqrsb(), 12,
               lambda kk: cq[:, kk * 1024:(kk + 1) * 1024],
               [f"cq{k}" for k in range(12)], qropev(), K_qrope)

    for s_ in range(4):
        r0 = 64 if s_ % 2 == 0 else 0
        b.op("y", lambda e, s_=s_, r0=r0: e.dma_start(
            out=qt[r0:r0 + 64, s_ * 1024:(s_ + 1) * 1024], in_=qropev()),
            reads=K_qrope, writes=[f"qt{s_}r"], dma="dQ")

    b.groupify("dQ")


    if stage == 4:
        emit_debug([
            ((lambda e: e.dma_start(out=dbg["cq"].ap(), in_=cq)),
             [f"cq{k}" for k in range(12)]),
            ((lambda e: e.dma_start(
                out=dbg["stg"].ap(),
                in_=stage_in.ap().rearrange("(a b) -> a b", b=1024))),
             ["stgin1", "stgin2"]),
            dbg_actA(), dbg_xres(),
        ])
        return nc, b

    ATTN_HEADS = 2 if stage == 8 else NH

    def emit_k_half(g):
        if g >= 2 * ATTN_HEADS:
            return
        h, half = g // 2, g % 2
        slot = g % 3
        crow = 0 if h % 2 == 0 else 64
        rrow = 64 - crow
        for r in range(4 * half, 4 * half + 4):
            b.op("y", lambda e, r=r, h=h, slot=slot, crow=crow: e.dma_start(
                out=kring(slot)[crow:crow + 64,
                                (r % 4) * 1024:(r % 4 + 1) * 1024],
                in_=AP(tensor=stage_out,
                       offset=r * C1N + (R_KC + h * 64) * 1024,
                       ap=[[1024, 64], [1, 1024]])),
                reads=["stgout"], writes=K_kring(slot), dma="dK%d" % slot)
            b.op("y", lambda e, r=r, slot=slot, rrow=rrow: e.dma_start(
                out=kring(slot)[rrow:rrow + 64,
                                (r % 4) * 1024:(r % 4 + 1) * 1024],
                in_=AP(tensor=stage_out, offset=r * C1N + R_KR * 1024,
                       ap=[[1024, 64], [1, 1024]])),
                reads=["stgout"], writes=K_kring(slot), dma="dK%d" % slot)

    def emit_v_load(h):
        if h >= ATTN_HEADS:
            return
        sl = h % 2
        b.op("g", lambda e, sl=sl: e.memset(vslot(sl), 1.0),
             writes=K_vslot(sl))
        for r in range(NC):
            b.op("y", lambda e, r=r, h=h, sl=sl: e.dma_start(
                out=vslot(sl)[:, r * 520:(r + 1) * 520].rearrange(
                    "p (k j) -> p k j", j=65)[:, :, 0:64],
                in_=AP(tensor=stage_out2, offset=r * C2N + h * 64,
                       ap=[[2048, 128], [128 * 2048, 8], [1, 64]])),
                reads=["stgout2"], writes=K_vslot(sl), dma="dV%d" % sl)

    def emit_qpair_wstream(p):
        ip = p % 2
        wstream(
            [wuq.ap()[p, :, q * 512:(q + 1) * 512] for q in range(3)],
            [512] * 3, wbf[:, ip * 2048:ip * 2048 + 1536], f"wb{ip}", eng="v")

    def emit_qpair(p):
        ip = p % 2
        cqk = [f"cq{k}" for k in range(12)]
        for kk in range(12):
            for qb in range(2):
                last = (kk == 11 and qb == 1)
                b.op("t", lambda e, kk=kk, qb=qb, ip=ip: e.matmul(
                    pE[:, qb * 512:(qb + 1) * 512],
                    wbf[:, ip * 2048 + kk * 128:ip * 2048 + (kk + 1) * 128],
                    cq[:, kk * 1024 + qb * 512:kk * 1024 + (qb + 1) * 512],
                    start=(kk == 0), stop=(kk == 11)),
                    reads=[f"wb{ip}a", f"wb{ip}b"] + (cqk if last or kk == 0 else []),
                    writes=["pE"] if (last or (kk == 0 and qb == 0)) else [],
                    quiet=not last)
        s0, s1 = (2 * p) % 4, (2 * p + 1) % 4
        b.op("v", lambda e, s0=s0: e.tensor_copy(
            out=qt[0:64, s0 * 1024:(s0 + 1) * 1024], in_=pE[0:64, 0:1024]),
            reads=["pE", f"qt{s0}r"], writes=[f"qt{s0}"])
        b.op("v", lambda e, s1=s1: e.tensor_copy(
            out=qt[64:128, s1 * 1024:(s1 + 1) * 1024], in_=pE[64:128, 0:1024]),
            reads=["pE", f"qt{s1}r"], writes=[f"qt{s1}"])

    def emit_bias(h):
        s_ = h % 4
        b.op("v", lambda e, s_=s_: e.tensor_tensor(
            out=qsq[:, :], in0=qt[:, s_ * 1024:(s_ + 1) * 1024],
            in1=qt[:, s_ * 1024:(s_ + 1) * 1024], op=ALU.mult),
            reads=[f"qt{s_}", f"qt{s_}r"], writes=["qsq"])
        for qb in range(2):
            b.op("t", lambda e, qb=qb: e.matmul(
                pE[0:1, qb * 512:(qb + 1) * 512], onesb[:, 0:1],
                qsq[:, qb * 512:(qb + 1) * 512], start=True, stop=True),
                reads=["qsq", "K"], writes=["pE"])
        b.op("v", lambda e: e.reduce_max(
            out=sc[0:1, 0:1], in_=pE[0:1, 0:512], axis=AXX),
            reads=["pE"], writes=["sc"])
        b.op("v", lambda e: e.reduce_max(
            out=sc[0:1, 1:2], in_=pE[0:1, 512:1024], axis=AXX),
            reads=["pE"], writes=["sc"])
        b.op("v", lambda e: e.tensor_tensor(
            out=sc[0:1, 2:3], in0=sc[0:1, 0:1], in1=sc[0:1, 1:2], op=ALU.max),
            writes=["sc"])
        cb = (h % 2) * 64
        b.op("t", lambda e, cb=cb, h=h: e.matmul(
            pE[0:1, 0:1], onesb[cb:cb + 64, 0:1],
            msq[cb:cb + 64, h // 2:h // 2 + 1], start=True, stop=False),
            reads=["msq", "K", "sc"], writes=["pE"])
        b.op("t", lambda e: e.matmul(
            pE[0:1, 0:1], onesb[0:64, 0:1], msq[0:64, 16:17],
            start=False, stop=True), reads=["msq", "K"], writes=["pE"])
        b.op("v", lambda e: e.tensor_tensor(
            out=sc[0:1, 3:4], in0=pE[0:1, 0:1], in1=sc[0:1, 2:3], op=ALU.mult),
            reads=["pE"], writes=["sc"])
        b.op("s", lambda e: e.activation(
            out=sc[0:1, 4:5], in_=sc[0:1, 3:4], func=AF.Ln),
            reads=["sc"], writes=["sc2"])
        b.op("s", lambda e: e.activation(
            out=sc[0:1, 5:6], in_=sc[0:1, 4:5], func=AF.Exp, scale=0.5,
            bias=fcons[0:1, 0:1]), reads=["sc2", "K"], writes=["sc3"])
        b.op("v", lambda e: e.tensor_scalar(
            out=sc[0:1, 6:7], in0=sc[0:1, 5:6], scalar1=fcons[0:1, 8:9],
            op0=ALU.mult, scalar2=fcons[0:1, 9:10], op1=ALU.add),
            reads=["sc3", "K"], writes=["sc4"])
        b.op("t", lambda e: e.matmul(
            pE[:, 0:1], onesf[0:1, 0:128], sc[0:1, 6:7], start=True, stop=True),
            reads=["sc4", "K"], writes=["pE"])
        b.op("v", lambda e, h=h: e.tensor_copy(
            out=biasb[:, h % 4:h % 4 + 1], in_=pE[:, 0:1]),
            reads=["pE"], writes=[f"bb{h % 4}"])

    def emit_finalize(h):
        sl = h % 2
        b.op("v", lambda e: e.reciprocal(
            out=scr1[64:65, 0:1024], in_=pB[64:65, 0:1024]),
            reads=["pB0", "pB1"], writes=["scr1"])
        for qb in range(2):
            b.op("t", lambda e, qb=qb: e.matmul(
                pE[0:64, qb * 512:(qb + 1) * 512], onesf[64:65, 0:64],
                scr1[64:65, qb * 512:(qb + 1) * 512], start=True, stop=True),
                reads=["scr1", "K"], writes=["pE"])
        b.op("v", lambda e: e.tensor_copy(
            out=scr1[0:64, 0:1024], in_=pE[0:64, 0:1024]),
            reads=["pE"], writes=["scr1"])
        b.op("v", lambda e, sl=sl: e.tensor_tensor(
            out=tmphv(sl), in0=pB[0:64, 0:1024],
            in1=scr1[0:64, 0:1024], op=ALU.mult),
            reads=["pB0", "pB1", "scr1"], writes=K_tmph(sl))
        b.op("y", lambda e, h=h, sl=sl: e.dma_start(
            out=actA[(h % 2) * 64:(h % 2) * 64 + 64,
                     (h // 2) * 1024:(h // 2 + 1) * 1024],
            in_=tmphv(sl)),
            reads=K_tmph(sl), writes=[AC[h // 2]], dma="dH%d" % sl)

    for g_ in (0, 1, 2):
        emit_k_half(g_)
    emit_v_load(0)
    emit_v_load(1)
    emit_qpair_wstream(0)
    emit_qpair(0)
    emit_bias(0)
    emit_bias(1)
    if ATTN_HEADS > 1:
        emit_qpair_wstream(1)

    for h in range(ATTN_HEADS):
        s_ = h % 4
        sl = h % 2
        def s_ap(kc):
            m = kc % 3
            return pE[:, 0:1024] if m == 2 else pA[:, m * 1024:(m + 1) * 1024]

        def s_key(kc):
            m = kc % 3
            return "pE" if m == 2 else f"pA{m}"

        def emit_pv(kc):
            ph = kc % 2
            for qb in range(2):
                b.op("t", lambda e, kc=kc, qb=qb, ph=ph, sl=sl: e.matmul(
                    pB[0:65, qb * 512:(qb + 1) * 512],
                    vslot(sl)[:, kc * 65:kc * 65 + 65],
                    pt[:, ph * 1024 + qb * 512:ph * 1024 + (qb + 1) * 512],
                    start=(kc == 0), stop=(kc == 63)),
                    reads=[f"pt{ph}"] + K_vslot(sl),
                    writes=[f"pB{qb}"] if qb == 1 else ["pB0"],
                    quiet=(qb == 0))

        for kc in range(64):
            g_ = 2 * h + kc // 32
            slot = g_ % 3
            ph = kc % 2
            for qb in range(2):
                b.op("t", lambda e, kc=kc, qb=qb, slot=slot, s_=s_, sa=s_ap(kc): e.matmul(
                    sa[:, qb * 512:(qb + 1) * 512],
                    kring(slot)[:, (kc % 32) * 128:(kc % 32 + 1) * 128],
                    qt[:, s_ * 1024 + qb * 512:s_ * 1024 + (qb + 1) * 512],
                    start=True, stop=True),
                    reads=K_kring(slot) + [f"qt{s_}", f"qt{s_}r"],
                    writes=[s_key(kc)], quiet=(qb == 0))
            if kc >= 2:
                emit_pv(kc - 2)
            b.op("s", lambda e, ph=ph, s4=s_, sa=s_ap(kc): e.activation(
                out=pt[:, ph * 1024:(ph + 1) * 1024],
                in_=sa, func=AF.Exp,
                bias=biasb[:, s4:s4 + 1], scale=INV),
                reads=[s_key(kc), f"bb{s_}"], writes=[f"pt{ph}"])
            if kc == 31:
                emit_k_half(2 * h + 3)
            if h % 2 == 1 and stage != 8:
                p_ = (h + 1) // 2
                if kc == 8 and p_ < 16:
                    emit_qpair(p_)
                if kc == 24 and p_ < 16:
                    emit_bias(h + 1)
                if kc == 40 and p_ < 16:
                    emit_bias(h + 2)
                if kc == 48 and p_ + 1 < 16:
                    emit_qpair_wstream(p_ + 1)
        emit_pv(62)
        emit_pv(63)
        emit_k_half(2 * h + 4)
        emit_finalize(h)
        emit_v_load(h + 2)

    if stage == 8:
        emit_debug([
            ((lambda e: e.dma_start(out=dbg["k0"].ap(), in_=kring(0))),
             K_kring(0)),
            ((lambda e: e.dma_start(out=dbg["k1"].ap(), in_=kring(1))),
             K_kring(1)),
            ((lambda e: e.dma_start(out=dbg["qt"].ap(), in_=qt[:, 0:4096])),
             ["qt0", "qt0r", "qt1", "qt1r"]),
            ((lambda e: e.dma_start(out=dbg["pt"].ap(), in_=pt)),
             ["pt0", "pt1"]),
            ((lambda e: e.dma_start(out=dbg["bias"].ap(), in_=biasb)),
             ["bb0", "bb1"]),
            ((lambda e: e.dma_start(out=dbg["vs"].ap(), in_=vslot(0))),
             K_vslot(0)),
            ((lambda e: e.dma_start(out=dbg["ao"].ap(), in_=tmphv(0))),
             K_tmph(0)),
            ((lambda e: e.dma_start(out=dbg["ao1"].ap(), in_=tmphv(1))),
             K_tmph(1)),
            ((lambda e: e.dma_start(out=dbg["k2"].ap(), in_=kring(2))),
             K_kring(2)),
        ])
        return nc, b

    def wo_stream(c):
        ip = c % 2
        wstream_big(wo.ap()[c, :, :],
            2048, wbf[:, ip * 2048:(ip + 1) * 2048], f"wb{ip}")
    for c in range(16):
        ip = c % 2
        if c == 0:
            wo_stream(0)
        if c + 1 < 16:
            wo_stream(c + 1)
        for kk in range(16):
            for qb in range(2):
                last = (kk == 15 and qb == 1)
                b.op("t", lambda e, kk=kk, qb=qb, ip=ip: e.matmul(
                    pA[:, ip * 1024 + qb * 512:ip * 1024 + (qb + 1) * 512],
                    wbf[:, ip * 2048 + kk * 128:ip * 2048 + (kk + 1) * 128],
                    actA[:, kk * 1024 + qb * 512:kk * 1024 + (qb + 1) * 512],
                    start=(kk == 0), stop=(kk == 15)),
                    reads=[f"wb{ip}a", f"wb{ip}b"] + (AC if last or kk == 0 else []),
                    writes=[f"pA{ip}"] if (last or (kk == 0 and qb == 0)) else [],
                    quiet=not last)
        b.op("v", lambda e, c=c, ip=ip: e.tensor_tensor(
            out=xres[:, c * 1024:(c + 1) * 1024],
            in0=xres[:, c * 1024:(c + 1) * 1024],
            in1=pA[:, ip * 1024:(ip + 1) * 1024], op=ALU.add),
            reads=[f"pA{ip}"], writes=[XR[c]])

    if stage == 5:
        emit_debug([dbg_actA(), dbg_xres()])
        return nc, b

    emit_ln(gate=True)
    if stage == 6:
        emit_debug([dbg_actA(), dbg_mask(), dbg_xres()])
        return nc, b
    emit_ffn(final=True)

    fin = ("SO", b.cnt["SO"])
    b.op("y", lambda e: e.nop(), extra=[fin], dma="dC")
    return nc, b


def finish_build(nc, b):
    hs = {}
    for name in ["dC", "dF0", "dF1", "dQ", "dM", "dR", "dK0", "dK1", "dK2",
                 "dV0", "dV1", "dS0", "dS1", "dS2", "dS3", "dH0", "dH1",
                 "CC", "PE", "VE", "SE", "PO", "SO"]:
        hs[name] = nc.alloc_semaphore(name)
    block = nc.Block()
    block.__enter__()
    b.emit(block, hs)
    block.__exit__(None, None, None)
    return nc


def prep_shared(i):
    f32 = np.float32
    bf = ml_dtypes.bfloat16
    c = np.ascontiguousarray
    o = {}
    def img(W, nco, nki):
        return c(np.asarray(W).reshape(nco, 128, nki, 128)
                 .transpose(0, 3, 2, 1).reshape(nco, 128, nki * 128))
    o["wdq"] = img(i["L0_attn_W_dq"], 12, 16)
    o["wuq"] = img(i["L0_attn_W_uq"], 16, 12)
    o["wdkv"] = img(i["L0_attn_W_dkv"], 4, 16)
    o["wuk"] = img(i["L0_attn_W_uk"], 16, 4)
    o["wuv"] = c(np.asarray(i["L0_attn_W_uv"]).reshape(4, 512, 4, 128)
                 .transpose(0, 3, 2, 1).reshape(4, 128, 2048))
    o["wo"] = img(i["L0_attn_W_o"], 16, 16)
    o["expw"] = c(np.asarray(i["L0_ffn_expert_w"])
                  .reshape(E, 16, 128, 16, 128)
                  .transpose(0, 3, 2, 1, 4).reshape(E, 16, 128, 2048))
    Wqr = np.asarray(i["L0_attn_W_qr"])
    o["wqr"] = c(Wqr.T.reshape(12, 128, 64).transpose(1, 0, 2)
                 .reshape(128, 768))
    o["wqrs"] = c(Wqr.reshape(32, 2, DCQ)[:, ::-1, :].reshape(64, DCQ)
                  .T.reshape(12, 128, 64).transpose(1, 0, 2).reshape(128, 768))
    Wkr = np.asarray(i["L0_attn_W_kr"])
    o["wkr"] = c(Wkr.T.reshape(16, 128, 64).transpose(1, 0, 2)
                 .reshape(128, 1024))
    o["wkrs"] = c(Wkr.reshape(32, 2, D)[:, ::-1, :].reshape(64, D)
                  .T.reshape(16, 128, 64).transpose(1, 0, 2).reshape(128, 1024))
    o["gwH"] = c(np.asarray(i["L0_ffn_gate_w"]).reshape(16, 128, 8)
                 .transpose(1, 0, 2).reshape(128, 128))
    o["gbH"] = c(np.asarray(i["L0_ffn_gate_b"]).reshape(8, 1))
    o["ebH"] = c(np.asarray(i["L0_ffn_expert_b"]).reshape(E, 16, 128)
                 .transpose(2, 0, 1).reshape(128, 128))
    o["qnH"] = c(np.asarray(i["L0_attn_q_norm_w"]).reshape(12, 128).T)
    o["kvnH"] = c(np.asarray(i["L0_attn_kv_norm_w"]).reshape(4, 128).T)
    o["identH"] = np.eye(128, dtype=f32)
    o["onesfH"] = np.ones((128, 128), f32)
    o["onesbH"] = np.ones((128, 128), bf)
    fc = np.zeros((128, 12), f32)
    fc[:, 0] = math.log(INV)
    fc[:, 1] = 1e30
    fc[:, 2] = 0.25
    fc[:, 3] = 1e-5
    fc[:, 4] = float(np.finfo(np.float32).eps)
    fc[:, 5] = 1.0 / 2048.0
    fc[:, 6] = 1.0 / 1536.0
    fc[:, 7] = 1.0 / 512.0
    fc[:, 8] = -1.0
    fc[:, 9] = SHIFT
    o["fconsH"] = fc
    inv = 1.0 / (10000.0 ** (np.arange(0, 64, 2, dtype=f32) / 64.0))
    t = np.arange(T, dtype=f32)
    fr = np.outer(inv, t)
    Ct = np.zeros((64, T), f32)
    St = np.zeros((64, T), f32)
    Ct[0::2] = np.cos(fr)
    Ct[1::2] = np.cos(fr)
    St[0::2] = -np.sin(fr)
    St[1::2] = np.sin(fr)
    o["_C"] = Ct.astype(bf)
    o["_S"] = St.astype(bf)
    return o


def make_in_maps(inputs):
    sh = prep_shared(inputs)
    Cf = sh.pop("_C")
    Sf = sh.pop("_S")
    x = np.asarray(inputs["L0_attn_x"])[0]
    xTfull = np.ascontiguousarray(x.T)
    maps = []
    for r in range(NC):
        m = dict(sh)
        m["xT"] = np.ascontiguousarray(xTfull[:, r * TL:(r + 1) * TL])
        m["ropeCH"] = np.ascontiguousarray(Cf[:, r * TL:(r + 1) * TL])
        m["ropeSH"] = np.ascontiguousarray(Sf[:, r * TL:(r + 1) * TL])
        maps.append(m)
    return maps


def run(inputs, stage=7, sim=False):
    nc, b = build_program(stage)
    nc = finish_build(nc, b)
    maps = make_in_maps(inputs)
    names = ["out"]
    if stage == 8:
        names = ["dbg_k0", "dbg_k1", "dbg_qt", "dbg_pt", "dbg_bias", "dbg_vs",
                 "dbg_ao", "dbg_ao1", "dbg_k2"]
    elif stage < 7:
        names = ["dbg_actA", "dbg_mask", "dbg_xres"]
        if stage >= 4:
            names += ["dbg_cq", "dbg_stg"]
        if stage in (3, 4, 5):
            names.remove("dbg_mask")
    if sim:
        from concourse import bass_interp
        s = bass_interp.MultiCoreSim(nc, NC)
        for r in range(NC):
            for k, v in maps[r].items():
                s.cores[r].tensor(k)[:] = v
        s.simulate()
        return [{n: np.asarray(s.cores[r].mem_tensor(n)) for n in names}
                for r in range(NC)]
    res = run_bass_kernel_spmd(nc, maps, core_ids=list(range(NC)))
    return res.results


def kernel(**inputs):
    res = run(inputs, stage=7, sim=False)
    parts = [np.asarray(r["out"]).T for r in res]
    return np.concatenate(parts, axis=0)[None, :, :].astype(np.float32)


# version 12 (final)
# speedup vs baseline: 1.4297x; 1.0859x over previous
import math
import numpy as np
import ml_dtypes
import concourse.bass as bass
import concourse.mybir as mybir
from concourse.ap import AP
from concourse.bass_utils import run_bass_kernel_spmd

F32 = mybir.dt.float32
BF16 = mybir.dt.bfloat16
AF = mybir.ActivationFunctionType
ALU = mybir.AluOpType
AXX = mybir.AxisListType.X

NC = 8
T = 8192
TL = T // NC
D = 2048
NH = 32
DCQ = 1536
DC = 512
E = 8
INV = 1.0 / math.sqrt(128.0)
SHIFT = 40.0

R_KC = 0
R_KR = 2048
R_M = 2112
R_V = 2115
NROWS = 4163
N1 = NROWS * 1024
CC1_ROWS = 2115

ENG_SEM = {"t": "PE", "v": "VE", "s": "SE", "g": "PO"}


class Builder:
    def __init__(self):
        self.streams = {e: [] for e in ("t", "v", "s", "g", "y")}
        self.cnt = {}
        self.res = {}

    def _inc(self, name, by):
        self.cnt[name] = self.cnt.get(name, 0) + by
        return (name, self.cnt[name])

    def op(self, eng, fn, reads=(), writes=(), extra=(), dma="dW", sem=None,
           quiet=False):
        own = ENG_SEM.get(eng)
        waits = [w for w in extra if w]
        for r in reads:
            e_ = self.res.get(r)
            if e_ and e_[0]:
                waits.append(e_[0])
        for r in writes:
            e_ = self.res.get(r)
            if e_:
                if e_[0]:
                    waits.append(e_[0])
                waits.extend(e_[1])
        if eng == "t":
            waits = [w for w in waits if w[0] != "PE"]
        if quiet:
            self.streams[eng].append((fn, tuple(waits), None, 0))
            return None
        if sem is None:
            sem = dma if eng == "y" else own
        by = 16 if eng == "y" else 1
        tok = self._inc(sem, by)
        self.streams[eng].append((fn, tuple(waits), sem, by))
        for r in reads:
            self.res.setdefault(r, [None, []])[1].append(tok)
        for r in writes:
            self.res[r] = [tok, []]
        return tok

    def groupify(self, sem):
        tot = (sem, self.cnt.get(sem, 0))
        for r, e_ in self.res.items():
            if e_[0] and e_[0][0] == sem:
                e_[0] = tot
            e_[1] = [tot if t[0] == sem else t for t in e_[1]]

    def emit(self, block, hs):
        def mk(ename):
            waited = {}

            def _run(eng):
                for fn, waits, sem, by in self.streams[ename]:
                    for (s, v) in waits:
                        if waited.get(s, 0) < v:
                            eng.wait_ge(hs[s], v)
                            waited[s] = v
                    ins = fn(eng)
                    if sem is not None:
                        ins.then_inc(hs[sem], by)
            return _run
        block.tensor(mk("t"))
        block.vector(mk("v"))
        block.scalar(mk("s"))
        block.gpsimd(mk("g"))
        block.sync(mk("y"))


def build_program(stage=7):
    nc = bass.Bass("TRN2", target_bir_lowering=False)
    b = Builder()

    def din(name, shape, dt=F32):
        return nc.dram_tensor(name, shape, dt, kind="ExternalInput")

    xT = din("xT", [D, TL])
    wdq = din("wdq", [12, 128, 2048])
    wuq = din("wuq", [16, 128, DCQ])
    wdkv = din("wdkv", [4, 128, 2048])
    wuk = din("wuk", [16, 128, DC])
    wuv = din("wuv", [4, 128, 2048])
    wo = din("wo", [16, 128, 2048])
    expw = din("expw", [E, 16, 128, 2048])
    wqr = din("wqr", [128, 768])
    wqrs = din("wqrs", [128, 768])
    wkr = din("wkr", [128, 1024])
    wkrs = din("wkrs", [128, 1024])
    gwH = din("gwH", [128, 128])
    gbH = din("gbH", [8, 1])
    ebH = din("ebH", [128, 128])
    qnH = din("qnH", [128, 12])
    kvnH = din("kvnH", [128, 4])
    identH = din("identH", [128, 128])
    onesfH = din("onesfH", [128, 128])
    onesbH = din("onesbH", [128, 128], BF16)
    fconsH = din("fconsH", [128, 12])
    hselH = din("hselH", [32, 2048], BF16)
    ropeCH = din("ropeCH", [64, TL], BF16)
    ropeSH = din("ropeSH", [64, TL], BF16)

    out = nc.dram_tensor("out", [D, TL], F32, kind="ExternalOutput")
    dbg = {}
    if stage < 7 or stage == 8:
        dbg["actA"] = nc.dram_tensor("dbg_actA", [128, 16 * 1024], BF16,
                                     kind="ExternalOutput")
        dbg["mask"] = nc.dram_tensor("dbg_mask", [8, 1024], F32,
                                     kind="ExternalOutput")
        dbg["xres"] = nc.dram_tensor("dbg_xres", [128, 16 * 1024], F32,
                                     kind="ExternalOutput")
        if stage >= 4:
            dbg["cq"] = nc.dram_tensor("dbg_cq", [128, 12 * 1024], BF16,
                                       kind="ExternalOutput")
            dbg["stg"] = nc.dram_tensor("dbg_stg", [NROWS, 1024], BF16,
                                        kind="ExternalOutput")
        if stage == 8:
            dbg["k0"] = nc.dram_tensor("dbg_k0", [128, 4096], BF16,
                                       kind="ExternalOutput")
            dbg["k1"] = nc.dram_tensor("dbg_k1", [128, 4096], BF16,
                                       kind="ExternalOutput")
            dbg["qt"] = nc.dram_tensor("dbg_qt", [128, 4096], BF16,
                                       kind="ExternalOutput")
            dbg["pt"] = nc.dram_tensor("dbg_pt", [128, 2048], BF16,
                                       kind="ExternalOutput")
            dbg["bias"] = nc.dram_tensor("dbg_bias", [128, 4], F32,
                                         kind="ExternalOutput")
            dbg["vs"] = nc.dram_tensor("dbg_vs", [128, 4160], BF16,
                                       kind="ExternalOutput")
            dbg["ao"] = nc.dram_tensor("dbg_ao", [64, 1024], BF16,
                                       kind="ExternalOutput")
            dbg["ao1"] = nc.dram_tensor("dbg_ao1", [64, 1024], BF16,
                                        kind="ExternalOutput")
            dbg["k2"] = nc.dram_tensor("dbg_k2", [128, 4096], BF16,
                                       kind="ExternalOutput")

    stage_in = nc.dram_tensor("stage_in", [N1], BF16)
    C1N = CC1_ROWS * 1024
    C2N = (NROWS - CC1_ROWS) * 1024
    stage_out = nc.dram_tensor("stage_out", [NC * C1N], BF16, addr_space="Shared")
    stage_out2 = nc.dram_tensor("stage_out2", [NC * C2N], BF16, addr_space="Shared")

    def sba(name, shape, dt):
        return nc.alloc_sbuf_tensor(name, shape, dt).ap()

    xres = sba("xres", [128, 16 * 1024], F32)
    actA = sba("actA", [128, 16 * 1024], BF16)
    cq = sba("cq", [128, 12 * 1024], BF16)
    shAt = nc.alloc_sbuf_tensor("shA", [128, 24704], BF16)
    shA = shAt.ap()
    shAf = shA.bitcast(F32)
    qt = sba("qt", [128, 4 * 1024], BF16)
    pt = sba("pt", [128, 2 * 1024], BF16)
    qsq = sba("qsq", [128, 1024], BF16)
    wbf = sba("wbf", [128, 2 * 2048], BF16)
    onesb = sba("onesb", [128, 128], BF16)
    mglob = sba("mglob", [128, 24], BF16)
    msq = sba("msq", [128, 24], BF16)
    mloc = sba("mloc", [128, 24], BF16)
    mgat = sba("mgat", [128, 192], BF16)
    wf32 = sba("wf32", [128, 1024], F32)
    scr1 = sba("scr1", [128, 1024], F32)
    scr2 = sba("scr2", [128, 1024], F32)
    ebT = sba("ebT", [128, 128], F32)
    gwT = sba("gwT", [128, 128], F32)
    gbT = sba("gbT", [8, 1], F32)
    qnT = sba("qnT", [128, 12], F32)
    kvnT = sba("kvnT", [128, 4], F32)
    identT = sba("identT", [128, 128], F32)
    onesf = sba("onesf", [128, 128], F32)
    fcons = sba("fcons", [128, 12], F32)
    sc = sba("sc", [128, 16], F32)
    biasb = sba("biasb", [128, 4], F32)
    gtm = sba("gtm", [128, 64], F32)
    selm = sba("selm", [128, 64], F32)
    gscr = sba("gscr", [128, 64], F32)
    fscr = sba("fscr", [128, 24], F32)

    def bfv(off, plen, cols):
        return shA[0:plen, off:off + cols]

    def f32v(off, plen, cols):
        return shAf[0:plen, off // 2:off // 2 + cols]

    def U(off, cols):
        return [f"u{k}" for k in range(off // 1024, (off + cols + 1023) // 1024)]

    rowm = lambda: f32v(0, 1, 1024)
    K_rowm = U(0, 2048)
    rowsq = lambda: f32v(2048, 1, 1024)
    K_rowsq = U(2048, 2048)
    rowt = lambda: f32v(4096, 1, 1024)
    K_rowt = U(4096, 2048)
    glrow = lambda: f32v(6144, 8, 1024)
    K_glrow = U(6144, 2048)
    rowe = lambda: f32v(8192, 1, 1024)
    K_rowe = U(8192, 2048)
    maskbv = lambda s: bfv(10240 + s * 1024, 128, 1024)
    K_maskb = lambda s: U(10240 + s * 1024, 1024)
    qslot = lambda s: bfv(12288 + s * 1024, 128, 1024)
    K_qslot = lambda s: U(12288 + s * 1024, 1024)
    nrmT = lambda s: f32v(12288 + s * 2048, 128, 1024)
    NRM_KEYS = [U(12288, 2048), U(14336, 2048)]
    ropeC = lambda: bfv(0, 64, 1024)
    K_ropeC = U(0, 1024)
    ropeS = lambda: bfv(1024, 64, 1024)
    K_ropeS = U(1024, 1024)
    kropev = lambda: bfv(2048, 64, 1024)
    K_krope = U(2048, 1024)
    kconv = lambda s: bfv(3072 + s * 1024, 128, 1024)
    K_kcon = lambda s: U(3072 + s * 1024, 1024)
    qropev = lambda: bfv(5120, 64, 1024)
    K_qrope = U(5120, 1024)
    wqrb = lambda: bfv(6144, 128, 768)
    K_wqrb = U(6144, 768)
    wqrsb = lambda: bfv(6912, 128, 768)
    K_wqrsb = U(6912, 768)
    wkrb = lambda: bfv(7680, 128, 1024)
    K_wkrb = U(7680, 1024)
    wkrsb = lambda: bfv(8704, 128, 1024)
    K_wkrsb = U(8704, 1024)
    ckvv = lambda: bfv(9728, 128, 4096)
    K_ckv = U(9728, 4096)
    rowP = lambda: f32v(13824, 1, 1024)
    ROWP_KEYS = U(13824, 2048)
    tmphv = lambda s: bfv(14336 + s * 1024, 64, 1024)
    K_tmph = lambda s: U(14336 + s * 1024, 1024)
    kring = lambda s: bfv(s * 4096, 128, 4096)
    wbig = lambda i: f32v(16384 + i * 4160, 128, 2048)
    K_kring = lambda s: U(s * 4096, 4096)
    vslot = lambda s: bfv(16384 + s * 4160, 128, 4160)
    K_vslot = lambda s: [f"vs{s}"]

    pA = nc.alloc_psum_tensor("pA", [128, 2048], F32).ap()
    pB = nc.alloc_psum_tensor("pB", [128, 1024], F32).ap()
    pE = nc.alloc_psum_tensor("pE", [128, 1024], F32).ap()

    XR = [f"xr{c}" for c in range(16)]
    AC = [f"ac{c}" for c in range(16)]

    for (d_, s_) in [(ebT, ebH), (gwT, gwH), (gbT, gbH), (qnT, qnH),
                     (kvnT, kvnH), (identT, identH), (onesf, onesfH),
                     (onesb, onesbH), (fcons, fconsH)]:
        b.op("y", lambda e, d=d_, s=s_: e.dma_start(out=d, in_=s.ap()),
             writes=["K"], dma="dC")
    b.op("y", lambda e: e.dma_start(
        out=xres.rearrange("p (c t) -> p c t", t=1024),
        in_=xT.ap().rearrange("(c p) t -> p c t", p=128)),
        writes=XR, dma="dC")
    b.groupify("dC")
    b.op("g", lambda e: e.memset(mloc[:, :], 0.0), writes=["mloc"])

    _wb = {"i": 0}

    def wstream_big(srcap, ncols, slot_ap, slot_key, cast="gv"):
        assert not isinstance(slot_key, list)
        hp = _wb["i"] % 2
        _wb["i"] += 1
        wk = f"vs{hp}"
        b.op("y", lambda e, src=srcap, n=ncols, hp=hp: e.dma_start(
            out=wbig(hp)[:, 0:n], in_=src), writes=[wk], dma=f"dF{hp}")
        if cast == "v":
            b.op("v", lambda e, n=ncols, sl=slot_ap, hp=hp: e.tensor_copy(
                out=sl[:, 0:n], in_=wbig(hp)[:, 0:n]),
                reads=[wk], writes=[slot_key + "a", slot_key + "b"])
            return
        h = ncols // 2
        b.op("g", lambda e, h=h, sl=slot_ap, hp=hp: e.tensor_copy(
            out=sl[:, 0:h], in_=wbig(hp)[:, 0:h]),
            reads=[wk], writes=[slot_key + "a"])
        b.op("v", lambda e, h=h, n=ncols, sl=slot_ap, hp=hp: e.tensor_copy(
            out=sl[:, h:n], in_=wbig(hp)[:, h:n]),
            reads=[wk], writes=[slot_key + "b"])

    def wstream(srcaps, ncols_each, slot_ap, slot_key, eng="g"):
        keys = slot_key if isinstance(slot_key, list) else [slot_key + "a"]
        coff = 0
        for srcap, ncols in zip(srcaps, ncols_each):
            assert ncols <= 512, ncols
            hp = (coff // 512) % 2
            wk = f"wf32{hp}"
            b.op("y", lambda e, s=srcap, n=ncols, hp=hp: e.dma_start(
                out=wf32[:, hp * 512:hp * 512 + n], in_=s),
                writes=[wk], dma=f"dF{hp}")
            b.op(eng, lambda e, n=ncols, co=coff, sl=slot_ap, hp=hp: e.tensor_copy(
                out=sl[:, co:co + n], in_=wf32[:, hp * 512:hp * 512 + n]),
                reads=[wk], writes=keys)
            coff += ncols

    def bcast_cols(row_ap, row_key, dst_ap, dst_key, psum_key="pA1"):
        keys = row_key if isinstance(row_key, list) else [row_key]
        for qb in range(2):
            b.op("t", lambda e, qb=qb: e.matmul(
                pA[:, 1024 + qb * 512:1024 + (qb + 1) * 512],
                onesf[0:1, 0:128], row_ap[0:1, qb * 512:(qb + 1) * 512],
                start=True, stop=True), reads=keys + ["K"], writes=[psum_key])
        b.op("v", lambda e: e.tensor_copy(out=dst_ap, in_=pA[:, 1024:2048]),
             reads=[psum_key], writes=[dst_key])

    def emit_ln(gate):
        for c in range(16):
            b.op("v", lambda e, c=c: e.tensor_copy(
                out=actA[:, c * 1024:(c + 1) * 1024],
                in_=xres[:, c * 1024:(c + 1) * 1024]),
                reads=[XR[c], "K"], writes=[AC[c]])
        for c in range(16):
            for qb in range(2):
                b.op("t", lambda e, c=c, qb=qb: e.matmul(
                    pE[0:1, qb * 512:(qb + 1) * 512], onesb[:, 0:1],
                    actA[:, c * 1024 + qb * 512:c * 1024 + (qb + 1) * 512],
                    start=(c == 0), stop=(c == 15)),
                    reads=[AC[c], "K"], writes=["pE"])
            b.op("v", lambda e, c=c: e.tensor_tensor(
                out=qslot(c % 2), in0=actA[:, c * 1024:(c + 1) * 1024],
                in1=actA[:, c * 1024:(c + 1) * 1024], op=ALU.mult),
                reads=[AC[c]], writes=K_qslot(c % 2))
            for qb in range(2):
                b.op("t", lambda e, c=c, qb=qb: e.matmul(
                    pB[0:1, qb * 512:(qb + 1) * 512], onesb[:, 0:1],
                    qslot(c % 2)[:, qb * 512:(qb + 1) * 512],
                    start=(c == 0), stop=(c == 15)),
                    reads=K_qslot(c % 2) + ["K"], writes=[f"pB{qb}"])
        b.op("v", lambda e: e.tensor_scalar(
            out=rowm(), in0=pE[0:1, 0:1024], scalar1=fcons[0:1, 5:6],
            scalar2=None, op0=ALU.mult), reads=["pE", "K"], writes=K_rowm)
        b.op("v", lambda e: e.tensor_scalar(
            out=rowsq(), in0=pB[0:1, 0:1024], scalar1=fcons[0:1, 5:6],
            scalar2=None, op0=ALU.mult), reads=["pB0", "pB1", "K"],
            writes=K_rowsq)
        b.op("v", lambda e: e.tensor_tensor(
            out=rowt(), in0=rowm(), in1=rowm(), op=ALU.mult),
            reads=K_rowm, writes=K_rowt)
        b.op("v", lambda e: e.tensor_tensor(
            out=rowsq(), in0=rowsq(), in1=rowt(), op=ALU.subtract),
            reads=K_rowt, writes=K_rowsq)
        b.op("s", lambda e: e.activation(
            out=rowt(), in_=rowsq(), func=AF.Ln, bias=fcons[0:1, 3:4]),
            reads=K_rowsq + ["K"], writes=K_rowt)
        b.op("s", lambda e: e.activation(
            out=rowsq(), in_=rowt(), func=AF.Exp, scale=-0.5),
            reads=["rowt"], writes=K_rowsq)
        bcast_cols(rowm(), K_rowm, scr1[:, :], "scr1")
        bcast_cols(rowsq(), K_rowsq, scr2[:, :], "scr2")
        for c in range(16):
            nk = NRM_KEYS[c % 2]
            b.op("v", lambda e, c=c: e.tensor_tensor(
                out=nrmT(c % 2), in0=xres[:, c * 1024:(c + 1) * 1024],
                in1=scr1[:, :], op=ALU.subtract),
                reads=[XR[c], "scr1"], writes=nk)
            if gate:
                b.op("v", lambda e, c=c: e.tensor_tensor(
                    out=nrmT(c % 2), in0=nrmT(c % 2), in1=scr2[:, :],
                    op=ALU.mult), reads=["scr2"], writes=nk)
                for qb in range(2):
                    b.op("t", lambda e, c=c, qb=qb: e.matmul(
                        pA[0:8, qb * 512:(qb + 1) * 512],
                        gwT[:, c * 8:(c + 1) * 8],
                        nrmT(c % 2)[:, qb * 512:(qb + 1) * 512],
                        start=(c == 0), stop=(c == 15)),
                        reads=nk + ["K"], writes=["pA0"])
                b.op("v", lambda e, c=c: e.tensor_copy(
                    out=actA[:, c * 1024:(c + 1) * 1024], in_=nrmT(c % 2)),
                    reads=nk, writes=[AC[c]])
            else:
                b.op("v", lambda e, c=c: e.tensor_tensor(
                    out=actA[:, c * 1024:(c + 1) * 1024], in0=nrmT(c % 2),
                    in1=scr2[:, :], op=ALU.mult),
                    reads=nk + ["scr2"], writes=[AC[c]])
        if not gate:
            return
        b.op("s", lambda e: e.activation(
            out=glrow(), in_=pA[0:8, 0:1024], func=AF.Identity,
            bias=gbT[0:8, 0:1]), reads=["pA0", "K"], writes=K_glrow)
        for i in range(8):
            b.op("t", lambda e, i=i: e.transpose(
                pB[0:128, i * 8:(i + 1) * 8],
                glrow()[0:8, i * 128:(i + 1) * 128], identT[0:8, 0:8]),
                reads=K_glrow + ["K"], writes=["pB0"])
        b.op("v", lambda e: e.tensor_copy(out=gtm[:, :], in_=pB[:, 0:64]),
             reads=["pB0"], writes=["gtm"])
        b.op("v", lambda e: e.reduce_max(
            out=fscr[:, 0:8], in_=gtm[:, :].rearrange("p (a b) -> p a b", b=8),
            axis=AXX), reads=["gtm"], writes=["fscr1"])
        for i in range(8):
            b.op("v", lambda e, i=i: e.tensor_scalar(
                out=selm[:, i * 8:(i + 1) * 8], in0=gtm[:, i * 8:(i + 1) * 8],
                scalar1=fscr[:, i:i + 1], scalar2=None, op0=ALU.is_equal),
                reads=["gtm", "fscr1"], writes=["selm"])
        b.op("v", lambda e: e.tensor_scalar(
            out=gscr[:, :], in0=selm[:, :], scalar1=fcons[0:128, 1:2],
            scalar2=None, op0=ALU.mult), reads=["selm", "K"], writes=["gscr"])
        b.op("v", lambda e: e.tensor_tensor(
            out=gscr[:, :], in0=gtm[:, :], in1=gscr[:, :], op=ALU.subtract),
            reads=["gtm"], writes=["gscr"])
        b.op("v", lambda e: e.reduce_max(
            out=fscr[:, 8:16], in_=gscr[:, :].rearrange("p (a b) -> p a b", b=8),
            axis=AXX), reads=["gscr"], writes=["fscr2"])
        for i in range(8):
            b.op("v", lambda e, i=i: e.tensor_scalar(
                out=selm[:, i * 8:(i + 1) * 8], in0=gtm[:, i * 8:(i + 1) * 8],
                scalar1=fscr[:, 8 + i:9 + i], op0=ALU.is_ge,
                scalar2=fcons[0:128, 2:3], op1=ALU.mult),
                reads=["gtm", "fscr2", "K"], writes=["selm"])
        for i in range(8):
            b.op("t", lambda e, i=i: e.transpose(
                pB[0:8, i * 128:(i + 1) * 128], selm[:, i * 8:(i + 1) * 8],
                identT[:, :]), reads=["selm", "K"],
                writes=["pB0" if i < 4 else "pB1"])
        b.op("v", lambda e: e.tensor_copy(
            out=scr2[0:8, :], in_=pB[0:8, 0:1024]),
            reads=["pB0", "pB1"], writes=["scr2"])

    def emit_ffn(final=False):
        def ffn_stream(idx):
            e_, c = idx // 16, idx % 16
            ip = idx % 2
            wstream_big(expw.ap()[e_, c, :, :],
                        2048, wbf[:, ip * 2048:(ip + 1) * 2048], f"wb{ip}")
        for e_ in range(E):
            sl = e_ % 2
            b.op("y", lambda e, e_=e_: e.dma_start(
                out=rowe(), in_=scr2[e_:e_ + 1, :]),
                reads=["scr2"], writes=K_rowe, dma="dR")
            for qb in range(2):
                b.op("t", lambda e, qb=qb: e.matmul(
                    pE[:, qb * 512:(qb + 1) * 512], onesf[0:1, 0:128],
                    rowe()[0:1, qb * 512:(qb + 1) * 512], start=True, stop=True),
                    reads=K_rowe + ["K"], writes=["pE"])
            b.op("v", lambda e, sl=sl: e.tensor_copy(
                out=maskbv(sl), in_=pE[:, 0:1024]),
                reads=["pE"], writes=K_maskb(sl))
            for c in range(16):
                idx = e_ * 16 + c
                ip = idx % 2
                if idx == 0:
                    ffn_stream(0)
                if idx + 1 < E * 16:
                    ffn_stream(idx + 1)
                for kk in range(16):
                    for qb in range(2):
                        last = (kk == 15 and qb == 1)
                        b.op("t", lambda e, kk=kk, qb=qb, ip=ip: e.matmul(
                            pA[:, ip * 1024 + qb * 512:ip * 1024 + (qb + 1) * 512],
                            wbf[:, ip * 2048 + kk * 128:ip * 2048 + (kk + 1) * 128],
                            actA[:, kk * 1024 + qb * 512:kk * 1024 + (qb + 1) * 512],
                            start=(kk == 0), stop=(kk == 15)),
                            reads=[f"wb{ip}a", f"wb{ip}b"] + (AC if last or kk == 0 else []),
                            writes=[f"pA{ip}"] if (last or (kk == 0 and qb == 0)) else [],
                            quiet=not last)
                b.op("s", lambda e, idx=idx, ip=ip: e.activation(
                    out=qslot(ip), in_=pA[:, ip * 1024:(ip + 1) * 1024],
                    func=AF.Relu, bias=ebT[:, idx % 128:idx % 128 + 1]),
                    reads=[f"pA{ip}", "K"], writes=K_qslot(ip))
                b.op("v", lambda e, ip=ip, sl=sl: e.tensor_tensor(
                    out=pt[:, ip * 1024:(ip + 1) * 1024], in0=qslot(ip),
                    in1=maskbv(sl), op=ALU.mult),
                    reads=K_qslot(ip) + K_maskb(sl), writes=[f"pt{ip}"])
                b.op("v", lambda e, c=c, ip=ip: e.tensor_tensor(
                    out=xres[:, c * 1024:(c + 1) * 1024],
                    in0=xres[:, c * 1024:(c + 1) * 1024],
                    in1=pt[:, ip * 1024:(ip + 1) * 1024], op=ALU.add),
                    reads=[f"pt{ip}"], writes=[XR[c]])
        if final:
            for c in range(16):
                b.op("y", lambda e, c=c: e.dma_start(
                    out=out.ap()[c * 128:(c + 1) * 128, :],
                    in_=xres[:, c * 1024:(c + 1) * 1024]),
                    reads=[XR[c]], dma="SO")

    def emit_debug(items):
        for fn, reads in items:
            b.op("y", lambda e, fn=fn: fn(e), reads=reads, dma="SO")
        fin = ("SO", b.cnt.get("SO", 0))
        b.op("y", lambda e: e.nop(), extra=[fin], dma="dC")

    dbg_actA = lambda: (lambda e: e.dma_start(out=dbg["actA"].ap(), in_=actA), AC)
    dbg_mask = lambda: (lambda e: e.dma_start(out=dbg["mask"].ap(), in_=scr2[0:8, :]), ["scr2"])
    dbg_xres = lambda: (lambda e: e.dma_start(out=dbg["xres"].ap(), in_=xres), XR)

    emit_ln(gate=True)
    if stage == 1:
        emit_debug([dbg_actA(), dbg_mask(), dbg_xres()])
        return nc, b
    emit_ffn()
    if stage == 2:
        emit_debug([dbg_actA(), dbg_mask(), dbg_xres()])
        return nc, b
    emit_ln(gate=False)
    if stage == 3:
        emit_debug([dbg_actA(), dbg_xres()])
        return nc, b

    b.op("y", lambda e: e.dma_start(out=ropeC(), in_=ropeCH.ap()),
         writes=K_ropeC, dma="dQ")
    b.op("y", lambda e: e.dma_start(out=ropeS(), in_=ropeSH.ap()),
         writes=K_ropeS, dma="dQ")
    b.groupify("dQ")
    wstream([wqr.ap()[:, 0:512], wqr.ap()[:, 512:768]],
            [512, 256], wqrb(), K_wqrb)
    wstream([wqrs.ap()[:, 0:512], wqrs.ap()[:, 512:768]],
            [512, 256], wqrsb(), K_wqrsb)
    wstream([wkr.ap()[:, 0:512], wkr.ap()[:, 512:1024]],
            [512, 512], wkrb(), K_wkrb)
    wstream([wkrs.ap()[:, 0:512], wkrs.ap()[:, 512:1024]],
            [512, 512], wkrsb(), K_wkrsb)
    b.res.setdefault("glrow", [None, []])
    b.res.setdefault("rowe", [None, []])

    def rmsnorm_block(nchunks, wsrc, dest_fn, dest_key, gamma_col_fn, scol,
                      dest_units=None, weng="g", cast="gv"):
        def nrm_stream(c):
            ip = c % 2
            wstream_big(wsrc.ap()[c, :, :],
                2048, wbf[:, ip * 2048:(ip + 1) * 2048], f"wb{ip}", cast=cast)
        for c in range(nchunks):
            ip = c % 2
            if c == 0:
                nrm_stream(0)
            if c + 1 < nchunks:
                nrm_stream(c + 1)
            for kk in range(16):
                for qb in range(2):
                    last = (kk == 15 and qb == 1)
                    b.op("t", lambda e, kk=kk, qb=qb, ip=ip: e.matmul(
                        pA[:, ip * 1024 + qb * 512:ip * 1024 + (qb + 1) * 512],
                        wbf[:, ip * 2048 + kk * 128:ip * 2048 + (kk + 1) * 128],
                        actA[:, kk * 1024 + qb * 512:kk * 1024 + (qb + 1) * 512],
                        start=(kk == 0), stop=(kk == 15)),
                        reads=[f"wb{ip}a", f"wb{ip}b"] + (AC if last or kk == 0 else []),
                        writes=[f"pA{ip}"] if (last or (kk == 0 and qb == 0)) else [],
                        quiet=not last)
            du = dest_units(c) if dest_units else []
            b.op("v", lambda e, c=c, ip=ip: e.tensor_copy(
                out=dest_fn(c), in_=pA[:, ip * 1024:(ip + 1) * 1024]),
                reads=[f"pA{ip}"], writes=[f"{dest_key}{c}"] + du)
            b.op("v", lambda e, c=c, ip=ip: e.tensor_tensor(
                out=pt[:, ip * 1024:(ip + 1) * 1024], in0=dest_fn(c),
                in1=dest_fn(c), op=ALU.mult),
                reads=[f"{dest_key}{c}"], writes=[f"pt{ip}"])
            for qb in range(2):
                b.op("t", lambda e, c=c, qb=qb, ip=ip, n=nchunks: e.matmul(
                    pE[0:1, qb * 512:(qb + 1) * 512], onesb[:, 0:1],
                    pt[:, ip * 1024 + qb * 512:ip * 1024 + (qb + 1) * 512],
                    start=(c == 0), stop=(c == n - 1)),
                    reads=[f"pt{ip}", "K"], writes=["pE"])
        b.op("v", lambda e, scol=scol: e.tensor_scalar(
            out=rowP(), in0=pE[0:1, 0:1024], scalar1=fcons[0:1, scol:scol + 1],
            scalar2=None, op0=ALU.mult), reads=["pE", "K"], writes=ROWP_KEYS)
        b.op("s", lambda e: e.activation(
            out=scr1[0:1, :], in_=rowP(), func=AF.Ln, bias=fcons[0:1, 4:5]),
            reads=ROWP_KEYS + ["K"], writes=["scr1"])
        b.op("s", lambda e: e.activation(
            out=rowP(), in_=scr1[0:1, :], func=AF.Exp, scale=-0.5),
            reads=["scr1"], writes=ROWP_KEYS)
        for qb in range(2):
            b.op("t", lambda e, qb=qb: e.matmul(
                pA[:, 1024 + qb * 512:1024 + (qb + 1) * 512],
                onesf[0:1, 0:128], rowP()[0:1, qb * 512:(qb + 1) * 512],
                start=True, stop=True), reads=ROWP_KEYS + ["K"], writes=["pA1"])
        b.op("v", lambda e: e.tensor_copy(out=scr2[:, :], in_=pA[:, 1024:2048]),
             reads=["pA1"], writes=["scr2"])
        for c in range(nchunks):
            du = dest_units(c) if dest_units else []
            b.op("v", lambda e, c=c: e.tensor_tensor(
                out=dest_fn(c), in0=dest_fn(c), in1=scr2[:, :], op=ALU.mult),
                reads=["scr2"], writes=[f"{dest_key}{c}"] + du)
            b.op("v", lambda e, c=c: e.tensor_scalar(
                out=dest_fn(c), in0=dest_fn(c), scalar1=gamma_col_fn(c),
                scalar2=None, op0=ALU.mult), reads=["K"],
                writes=[f"{dest_key}{c}"] + du)

    def rope_build(wbk, wsbk, wb_ap, wsb_ap, nkk, rhs_fn, rhs_keys, dest_ap,
                   dest_keys):
        for kk in range(nkk):
            for qb in range(2):
                b.op("t", lambda e, kk=kk, qb=qb: e.matmul(
                    pA[0:64, qb * 512:(qb + 1) * 512],
                    wb_ap[:, kk * 64:(kk + 1) * 64],
                    rhs_fn(kk)[:, qb * 512:(qb + 1) * 512],
                    start=(kk == 0), stop=(kk == nkk - 1)),
                    reads=wbk + [rhs_keys[kk]], writes=["pA0"])
        for kk in range(nkk):
            for qb in range(2):
                b.op("t", lambda e, kk=kk, qb=qb: e.matmul(
                    pA[0:64, 1024 + qb * 512:1024 + (qb + 1) * 512],
                    wsb_ap[:, kk * 64:(kk + 1) * 64],
                    rhs_fn(kk)[:, qb * 512:(qb + 1) * 512],
                    start=(kk == 0), stop=(kk == nkk - 1)),
                    reads=wsbk + [rhs_keys[kk]], writes=["pA1"])
        b.op("v", lambda e: e.tensor_tensor(
            out=scr1[0:64, :], in0=pA[0:64, 0:1024], in1=ropeC(), op=ALU.mult),
            reads=["pA0"] + K_ropeC, writes=["scr1"])
        b.op("v", lambda e: e.tensor_tensor(
            out=scr2[0:64, :], in0=pA[0:64, 1024:2048], in1=ropeS(),
            op=ALU.mult), reads=["pA1"] + K_ropeS, writes=["scr2"])
        b.op("v", lambda e, d=dest_ap: e.tensor_tensor(
            out=d, in0=scr1[0:64, :], in1=scr2[0:64, :], op=ALU.add),
            reads=["scr1", "scr2"], writes=dest_keys)

    rmsnorm_block(4, wdkv, lambda c: ckvv()[:, c * 1024:(c + 1) * 1024], "ck",
                  lambda c: kvnT[:, c:c + 1], 7,
                  dest_units=lambda c: U(9728 + c * 1024, 1024))

    def kc_stream(c):
        ip = c % 2
        wstream_big(wuk.ap()[c, :, :],
                    512, wbf[:, ip * 2048:ip * 2048 + 512], f"wb{ip}")
    for c in range(16):
        ip = c % 2
        if c == 0:
            kc_stream(0)
        if c + 1 < 16:
            kc_stream(c + 1)
        for kk in range(4):
            for qb in range(2):
                b.op("t", lambda e, kk=kk, qb=qb, ip=ip: e.matmul(
                    pA[:, ip * 1024 + qb * 512:ip * 1024 + (qb + 1) * 512],
                    wbf[:, ip * 2048 + kk * 128:ip * 2048 + (kk + 1) * 128],
                    ckvv()[:, kk * 1024 + qb * 512:kk * 1024 + (qb + 1) * 512],
                    start=(kk == 0), stop=(kk == 3)),
                    reads=[f"wb{ip}a", f"wb{ip}b", f"ck{kk}"] + U(9728 + kk * 1024, 1024),
                    writes=[f"pA{ip}"])
        b.op("v", lambda e, ip=ip: e.tensor_copy(
            out=kconv(ip), in_=pA[:, ip * 1024:(ip + 1) * 1024]),
            reads=[f"pA{ip}"], writes=K_kcon(ip))
        b.op("v", lambda e, c=c, ip=ip: e.reduce_max(
            out=mloc[:, c:c + 1], in_=kconv(ip), axis=AXX,
            apply_absolute_value=True), reads=K_kcon(ip), writes=["mloc"])
        b.op("y", lambda e, c=c, ip=ip: e.dma_start(
            out=AP(tensor=stage_in, offset=(R_KC + c * 128) * 1024,
                   ap=[[1024, 128], [1, 1024]]),
            in_=kconv(ip)), reads=K_kcon(ip) + ["stgin1"], dma="dS%d" % ip)

    rope_build(K_wkrb, K_wkrsb, wkrb(), wkrsb(), 16,
               lambda kk: actA[:, kk * 1024:(kk + 1) * 1024],
               AC, kropev(), K_krope)
    b.op("v", lambda e: e.reduce_max(
        out=mloc[0:64, 16:17], in_=kropev(), axis=AXX,
        apply_absolute_value=True), reads=K_krope, writes=["mloc"])
    b.op("y", lambda e: e.dma_start(
        out=AP(tensor=stage_in, offset=R_KR * 1024, ap=[[1024, 64], [1, 1024]]),
        in_=kropev()), reads=K_krope + ["stgin1"], dma="dM")
    b.op("y", lambda e: e.dma_start(
        out=AP(tensor=stage_in, offset=R_M * 1024, ap=[[24, 128], [1, 24]]),
        in_=mloc[:, :]), reads=["mloc", "stgin1"], dma="dM")
    b.groupify("dM")

    cc1 = b.op("g", lambda e: e.collective_compute(
        "AllGather", ALU.bypass, replica_groups=[list(range(NC))],
        ins=[stage_in.ap()[0:CC1_ROWS * 1024]
             .rearrange("(a b) -> a b", b=1024).opt()],
        outs=[stage_out.ap().rearrange("(a b) -> a b", b=1024).opt()]),
        writes=["stgin1", "stgout"], sem="CC")

    def vb_stream(ob):
        ip = ob % 2
        wstream_big(wuv.ap()[ob, :, :],
                    2048, wbf[:, ip * 2048:(ip + 1) * 2048], f"wb{ip}", cast="v")
    for ob in range(4):
        ip = ob % 2
        if ob == 0:
            vb_stream(0)
        if ob + 1 < 4:
            vb_stream(ob + 1)
        for tt in range(8):
            tp = tt % 2
            for dc in range(4):
                b.op("t", lambda e, tt=tt, dc=dc, ip=ip, tp=tp: e.matmul(
                    pB[:, tp * 512:(tp + 1) * 512],
                    ckvv()[:, dc * 1024 + tt * 128:dc * 1024 + tt * 128 + 128],
                    wbf[:, ip * 2048 + dc * 512:ip * 2048 + (dc + 1) * 512],
                    start=(dc == 0), stop=(dc == 3)),
                    reads=[f"wb{ip}a", f"wb{ip}b", f"ck{dc}"] + U(9728 + dc * 1024, 1024),
                    writes=[f"pB{tp}"])
            b.op("v", lambda e, tp=tp: e.tensor_copy(
                out=pt[:, tp * 1024:tp * 1024 + 512],
                in_=pB[:, tp * 512:(tp + 1) * 512]),
                reads=[f"pB{tp}"], writes=[f"pt{tp}"])
            b.op("y", lambda e, tt=tt, ob=ob, tp=tp: e.dma_start(
                out=AP(tensor=stage_in,
                       offset=R_V * 1024 + tt * 128 * 2048 + ob * 512,
                       ap=[[2048, 128], [1, 512]]),
                in_=pt[:, tp * 1024:tp * 1024 + 512]),
                reads=[f"pt{tp}", "stgin2"], dma="dS%d" % (2 + tp))

    cc2 = b.op("g", lambda e: e.collective_compute(
        "AllGather", ALU.bypass, replica_groups=[list(range(NC))],
        ins=[stage_in.ap()[CC1_ROWS * 1024:]
             .rearrange("(a b) -> a b", b=1024).opt()],
        outs=[stage_out2.ap().rearrange("(a b) -> a b", b=1024).opt()]),
        writes=["stgin2", "stgout2"], sem="CC")
    rmsnorm_block(12, wdq, lambda c: cq[:, c * 1024:(c + 1) * 1024], "cq",
                  lambda c: qnT[:, c:c + 1], 6, cast="v")
    rope_build(K_wqrb, K_wqrsb, wqrb(), wqrsb(), 12,
               lambda kk: cq[:, kk * 1024:(kk + 1) * 1024],
               [f"cq{k}" for k in range(12)], qropev(), K_qrope)

    for r in range(NC):
        b.op("y", lambda e, r=r: e.dma_start(
            out=mgat[:, r * 24:(r + 1) * 24],
            in_=AP(tensor=stage_out, offset=r * C1N + R_M * 1024,
                   ap=[[24, 128], [1, 24]])),
            reads=["stgout"], writes=["mgat"], dma="dM")
    b.groupify("dM")
    b.op("v", lambda e: e.reduce_max(
        out=mglob[:, :], in_=mgat[:, :].rearrange("p (r c) -> p c r", c=24),
        axis=AXX), reads=["mgat"], writes=["mglob"])
    b.op("v", lambda e: e.tensor_tensor(
        out=msq[:, :], in0=mglob[:, :], in1=mglob[:, :], op=ALU.mult),
        reads=["mglob"], writes=["msq"])

    for s_ in range(4):
        r0 = 64 if s_ % 2 == 0 else 0
        b.op("y", lambda e, s_=s_, r0=r0: e.dma_start(
            out=qt[r0:r0 + 64, s_ * 1024:(s_ + 1) * 1024], in_=qropev()),
            reads=K_qrope, writes=[f"qt{s_}r"], dma="dQ")

    b.groupify("dQ")


    if stage == 4:
        emit_debug([
            ((lambda e: e.dma_start(out=dbg["cq"].ap(), in_=cq)),
             [f"cq{k}" for k in range(12)]),
            ((lambda e: e.dma_start(
                out=dbg["stg"].ap(),
                in_=stage_in.ap().rearrange("(a b) -> a b", b=1024))),
             ["stgin1", "stgin2"]),
            dbg_actA(), dbg_xres(),
        ])
        return nc, b

    ATTN_HEADS = 2 if stage == 8 else NH

    def emit_k_half(g):
        if g >= 2 * ATTN_HEADS:
            return
        h, half = g // 2, g % 2
        slot = g % 3
        crow = 0 if h % 2 == 0 else 64
        rrow = 64 - crow
        for r in range(4 * half, 4 * half + 4):
            b.op("y", lambda e, r=r, h=h, slot=slot, crow=crow: e.dma_start(
                out=kring(slot)[crow:crow + 64,
                                (r % 4) * 1024:(r % 4 + 1) * 1024],
                in_=AP(tensor=stage_out,
                       offset=r * C1N + (R_KC + h * 64) * 1024,
                       ap=[[1024, 64], [1, 1024]])),
                reads=["stgout"], writes=K_kring(slot), dma="dK%d" % slot)
            b.op("y", lambda e, r=r, slot=slot, rrow=rrow: e.dma_start(
                out=kring(slot)[rrow:rrow + 64,
                                (r % 4) * 1024:(r % 4 + 1) * 1024],
                in_=AP(tensor=stage_out, offset=r * C1N + R_KR * 1024,
                       ap=[[1024, 64], [1, 1024]])),
                reads=["stgout"], writes=K_kring(slot), dma="dK%d" % slot)

    def emit_v_load(h):
        if h >= ATTN_HEADS:
            return
        sl = h % 2
        b.op("g", lambda e, sl=sl: e.memset(vslot(sl), 1.0),
             writes=K_vslot(sl))
        for r in range(NC):
            b.op("y", lambda e, r=r, h=h, sl=sl: e.dma_start(
                out=vslot(sl)[:, r * 520:(r + 1) * 520].rearrange(
                    "p (k j) -> p k j", j=65)[:, :, 0:64],
                in_=AP(tensor=stage_out2, offset=r * C2N + h * 64,
                       ap=[[2048, 128], [128 * 2048, 8], [1, 64]])),
                reads=["stgout2"], writes=K_vslot(sl), dma="dV%d" % sl)

    def emit_qpair_wstream(p):
        ip = p % 2
        wstream(
            [wuq.ap()[p, :, q * 512:(q + 1) * 512] for q in range(3)],
            [512] * 3, wbf[:, ip * 2048:ip * 2048 + 1536], f"wb{ip}", eng="v")

    def emit_qpair(p):
        ip = p % 2
        cqk = [f"cq{k}" for k in range(12)]
        for kk in range(12):
            for qb in range(2):
                last = (kk == 11 and qb == 1)
                b.op("t", lambda e, kk=kk, qb=qb, ip=ip: e.matmul(
                    pE[:, qb * 512:(qb + 1) * 512],
                    wbf[:, ip * 2048 + kk * 128:ip * 2048 + (kk + 1) * 128],
                    cq[:, kk * 1024 + qb * 512:kk * 1024 + (qb + 1) * 512],
                    start=(kk == 0), stop=(kk == 11)),
                    reads=[f"wb{ip}a", f"wb{ip}b"] + (cqk if last or kk == 0 else []),
                    writes=["pE"] if (last or (kk == 0 and qb == 0)) else [],
                    quiet=not last)
        s0, s1 = (2 * p) % 4, (2 * p + 1) % 4
        b.op("v", lambda e, s0=s0: e.tensor_copy(
            out=qt[0:64, s0 * 1024:(s0 + 1) * 1024], in_=pE[0:64, 0:1024]),
            reads=["pE", f"qt{s0}r"], writes=[f"qt{s0}"])
        b.op("v", lambda e, s1=s1: e.tensor_copy(
            out=qt[64:128, s1 * 1024:(s1 + 1) * 1024], in_=pE[64:128, 0:1024]),
            reads=["pE", f"qt{s1}r"], writes=[f"qt{s1}"])

    def emit_bias(h, part="ab"):
        s_ = h % 4
        if "a" not in part:
            _bias_b(h)
            return
        b.op("v", lambda e, s_=s_: e.tensor_tensor(
            out=qsq[:, :], in0=qt[:, s_ * 1024:(s_ + 1) * 1024],
            in1=qt[:, s_ * 1024:(s_ + 1) * 1024], op=ALU.mult),
            reads=[f"qt{s_}", f"qt{s_}r"], writes=["qsq"])
        for qb in range(2):
            b.op("t", lambda e, qb=qb: e.matmul(
                pE[0:1, qb * 512:(qb + 1) * 512], onesb[:, 0:1],
                qsq[:, qb * 512:(qb + 1) * 512], start=True, stop=True),
                reads=["qsq", "K"], writes=["pE"])
        b.op("v", lambda e: e.reduce_max(
            out=sc[0:1, 0:1], in_=pE[0:1, 0:512], axis=AXX),
            reads=["pE"], writes=["sc"])
        b.op("v", lambda e: e.reduce_max(
            out=sc[0:1, 1:2], in_=pE[0:1, 512:1024], axis=AXX),
            reads=["pE"], writes=["sc"])
        b.op("v", lambda e: e.tensor_tensor(
            out=sc[0:1, 2:3], in0=sc[0:1, 0:1], in1=sc[0:1, 1:2], op=ALU.max),
            writes=["sc"])
        cb = (h % 2) * 64
        b.op("t", lambda e, cb=cb, h=h: e.matmul(
            pE[0:1, 0:1], onesb[cb:cb + 64, 0:1],
            msq[cb:cb + 64, h // 2:h // 2 + 1], start=True, stop=False),
            reads=["msq", "K", "sc"], writes=["pE"])
        b.op("t", lambda e: e.matmul(
            pE[0:1, 0:1], onesb[0:64, 0:1], msq[0:64, 16:17],
            start=False, stop=True), reads=["msq", "K"], writes=["pE"])
        b.op("v", lambda e: e.tensor_tensor(
            out=sc[0:1, 3:4], in0=pE[0:1, 0:1], in1=sc[0:1, 2:3], op=ALU.mult),
            reads=["pE"], writes=["sc"])
        b.op("s", lambda e: e.activation(
            out=sc[0:1, 4:5], in_=sc[0:1, 3:4], func=AF.Ln),
            reads=["sc"], writes=["sc2"])
        b.op("s", lambda e: e.activation(
            out=sc[0:1, 5:6], in_=sc[0:1, 4:5], func=AF.Exp, scale=0.5,
            bias=fcons[0:1, 0:1]), reads=["sc2", "K"], writes=["sc3"])
        b.op("v", lambda e: e.tensor_scalar(
            out=sc[0:1, 6:7], in0=sc[0:1, 5:6], scalar1=fcons[0:1, 8:9],
            op0=ALU.mult, scalar2=fcons[0:1, 9:10], op1=ALU.add),
            reads=["sc3", "K"], writes=["sc4"])
        if "b" in part:
            _bias_b(h)

    def _bias_b(h):
        b.op("t", lambda e: e.matmul(
            pE[:, 0:1], onesf[0:1, 0:128], sc[0:1, 6:7], start=True, stop=True),
            reads=["sc4", "K"], writes=["pE"])
        b.op("v", lambda e, h=h: e.tensor_copy(
            out=biasb[:, h % 4:h % 4 + 1], in_=pE[:, 0:1]),
            reads=["pE"], writes=[f"bb{h % 4}"])

    def emit_finalize(h):
        sl = h % 2
        b.op("v", lambda e: e.tensor_copy(
            out=scr1[0:65, 0:1024], in_=pB[0:65, 0:1024]),
            reads=["pB0", "pB1"], writes=["scr1"])
        b.op("v", lambda e, sl=sl: e.tensor_copy(
            out=tmphv(sl), in_=scr1[0:64, 0:1024]),
            reads=["scr1"], writes=K_tmph(sl))
        b.op("y", lambda e, h=h: e.dma_start(
            out=scr2[h:h + 1, :], in_=scr1[64:65, 0:1024]),
            reads=["scr1"], writes=["scr2"], dma="dR")
        b.op("y", lambda e, h=h, sl=sl: e.dma_start(
            out=actA[(h % 2) * 64:(h % 2) * 64 + 64,
                     (h // 2) * 1024:(h // 2 + 1) * 1024],
            in_=tmphv(sl)),
            reads=K_tmph(sl), writes=[AC[h // 2]], dma="dH%d" % sl)

    for g_ in (0, 1, 2):
        emit_k_half(g_)
    emit_v_load(0)
    emit_v_load(1)
    emit_qpair_wstream(0)
    emit_qpair(0)
    emit_bias(0)
    emit_bias(1)
    if ATTN_HEADS > 1:
        emit_qpair_wstream(1)

    for h in range(ATTN_HEADS):
        s_ = h % 4
        sl = h % 2
        def s_ap(kc):
            m = kc % 3
            return pE[:, 0:1024] if m == 2 else pA[:, m * 1024:(m + 1) * 1024]

        def s_key(kc):
            m = kc % 3
            return "pE" if m == 2 else f"pA{m}"

        def emit_pv(kc):
            ph = kc % 2
            for qb in range(2):
                b.op("t", lambda e, kc=kc, qb=qb, ph=ph, sl=sl: e.matmul(
                    pB[0:65, qb * 512:(qb + 1) * 512],
                    vslot(sl)[:, kc * 65:kc * 65 + 65],
                    pt[:, ph * 1024 + qb * 512:ph * 1024 + (qb + 1) * 512],
                    start=(kc == 0), stop=(kc == 63)),
                    reads=[f"pt{ph}"] + K_vslot(sl),
                    writes=[f"pB{qb}"] if qb == 1 else ["pB0"],
                    quiet=(qb == 0))

        for kc in range(64):
            g_ = 2 * h + kc // 32
            slot = g_ % 3
            ph = kc % 2
            for qb in range(2):
                b.op("t", lambda e, kc=kc, qb=qb, slot=slot, s_=s_, sa=s_ap(kc): e.matmul(
                    sa[:, qb * 512:(qb + 1) * 512],
                    kring(slot)[:, (kc % 32) * 128:(kc % 32 + 1) * 128],
                    qt[:, s_ * 1024 + qb * 512:s_ * 1024 + (qb + 1) * 512],
                    start=True, stop=True),
                    reads=K_kring(slot) + [f"qt{s_}", f"qt{s_}r"],
                    writes=[s_key(kc)], quiet=(qb == 0))
            if kc >= 2:
                emit_pv(kc - 2)
            b.op("s", lambda e, ph=ph, s4=s_, sa=s_ap(kc): e.activation(
                out=pt[:, ph * 1024:(ph + 1) * 1024],
                in_=sa, func=AF.Exp,
                bias=biasb[:, s4:s4 + 1], scale=INV),
                reads=[s_key(kc), f"bb{s_}"], writes=[f"pt{ph}"])
            if kc == 31:
                emit_k_half(2 * h + 3)
            if h % 2 == 1 and stage != 8:
                p_ = (h + 1) // 2
                if kc == 8 and p_ < 16:
                    emit_qpair(p_)
                if kc == 24 and p_ < 16:
                    emit_bias(h + 1, "a")
                if kc == 32 and p_ < 16:
                    emit_bias(h + 1, "b")
                if kc == 40 and p_ < 16:
                    emit_bias(h + 2, "a")
                if kc == 48 and p_ < 16:
                    emit_bias(h + 2, "b")
                if kc == 48 and p_ + 1 < 16:
                    emit_qpair_wstream(p_ + 1)
        emit_pv(62)
        emit_pv(63)
        emit_k_half(2 * h + 4)
        emit_finalize(h)
        emit_v_load(h + 2)

    if stage == 8:
        emit_debug([
            ((lambda e: e.dma_start(out=dbg["k0"].ap(), in_=kring(0))),
             K_kring(0)),
            ((lambda e: e.dma_start(out=dbg["k1"].ap(), in_=kring(1))),
             K_kring(1)),
            ((lambda e: e.dma_start(out=dbg["qt"].ap(), in_=qt[:, 0:4096])),
             ["qt0", "qt0r", "qt1", "qt1r"]),
            ((lambda e: e.dma_start(out=dbg["pt"].ap(), in_=pt)),
             ["pt0", "pt1"]),
            ((lambda e: e.dma_start(out=dbg["bias"].ap(), in_=biasb)),
             ["bb0", "bb1"]),
            ((lambda e: e.dma_start(out=dbg["vs"].ap(), in_=vslot(0))),
             K_vslot(0)),
            ((lambda e: e.dma_start(out=dbg["ao"].ap(), in_=tmphv(0))),
             K_tmph(0)),
            ((lambda e: e.dma_start(out=dbg["ao1"].ap(), in_=tmphv(1))),
             K_tmph(1)),
            ((lambda e: e.dma_start(out=dbg["k2"].ap(), in_=kring(2))),
             K_kring(2)),
        ])
        return nc, b

    b.op("y", lambda e: e.dma_start(out=cq[0:32, 0:2048], in_=hselH.ap()),
         writes=["cq0", "cq1"], dma="dC")
    b.op("v", lambda e: e.reciprocal(
        out=wf32[0:32, 0:1024], in_=scr2[0:32, 0:1024]),
        reads=["scr2"], writes=["wf320", "wf321"])
    b.op("v", lambda e: e.tensor_copy(
        out=qsq[0:32, 0:1024], in_=wf32[0:32, 0:1024]),
        reads=["wf320", "wf321"], writes=["qsq"])
    for c in range(16):
        pk = ["pA0", "pA1", "pE"][c % 3]
        po = (pA[:, 0:1024], pA[:, 1024:2048], pE[:, 0:1024])[c % 3]
        for qb in range(2):
            b.op("t", lambda e, c=c, qb=qb, po=po: e.matmul(
                po[:, qb * 512:(qb + 1) * 512],
                cq[0:32, c * 128:(c + 1) * 128],
                qsq[0:32, qb * 512:(qb + 1) * 512], start=True, stop=True),
                reads=["qsq", "cq0", "cq1"], writes=[pk])
        b.op("v", lambda e, c=c, po=po, pk=pk: e.tensor_tensor(
            out=actA[:, c * 1024:(c + 1) * 1024],
            in0=actA[:, c * 1024:(c + 1) * 1024],
            in1=po, op=ALU.mult), reads=[pk, AC[c]], writes=[AC[c]])

    def wo_stream(c):
        ip = c % 2
        wstream_big(wo.ap()[c, :, :],
            2048, wbf[:, ip * 2048:(ip + 1) * 2048], f"wb{ip}")
    for c in range(16):
        ip = c % 2
        if c == 0:
            wo_stream(0)
        if c + 1 < 16:
            wo_stream(c + 1)
        for kk in range(16):
            for qb in range(2):
                last = (kk == 15 and qb == 1)
                b.op("t", lambda e, kk=kk, qb=qb, ip=ip: e.matmul(
                    pA[:, ip * 1024 + qb * 512:ip * 1024 + (qb + 1) * 512],
                    wbf[:, ip * 2048 + kk * 128:ip * 2048 + (kk + 1) * 128],
                    actA[:, kk * 1024 + qb * 512:kk * 1024 + (qb + 1) * 512],
                    start=(kk == 0), stop=(kk == 15)),
                    reads=[f"wb{ip}a", f"wb{ip}b"] + (AC if last or kk == 0 else []),
                    writes=[f"pA{ip}"] if (last or (kk == 0 and qb == 0)) else [],
                    quiet=not last)
        b.op("v", lambda e, c=c, ip=ip: e.tensor_tensor(
            out=xres[:, c * 1024:(c + 1) * 1024],
            in0=xres[:, c * 1024:(c + 1) * 1024],
            in1=pA[:, ip * 1024:(ip + 1) * 1024], op=ALU.add),
            reads=[f"pA{ip}"], writes=[XR[c]])

    if stage == 5:
        emit_debug([dbg_actA(), dbg_xres()])
        return nc, b

    emit_ln(gate=True)
    if stage == 6:
        emit_debug([dbg_actA(), dbg_mask(), dbg_xres()])
        return nc, b
    emit_ffn(final=True)

    fin = ("SO", b.cnt["SO"])
    b.op("y", lambda e: e.nop(), extra=[fin], dma="dC")
    return nc, b


def finish_build(nc, b):
    hs = {}
    for name in ["dC", "dF0", "dF1", "dQ", "dM", "dR", "dK0", "dK1", "dK2",
                 "dV0", "dV1", "dS0", "dS1", "dS2", "dS3", "dH0", "dH1",
                 "CC", "PE", "VE", "SE", "PO", "SO"]:
        hs[name] = nc.alloc_semaphore(name)
    block = nc.Block()
    block.__enter__()
    b.emit(block, hs)
    block.__exit__(None, None, None)
    return nc


def prep_shared(i):
    f32 = np.float32
    bf = ml_dtypes.bfloat16
    c = np.ascontiguousarray
    o = {}
    def img(W, nco, nki):
        return c(np.asarray(W).reshape(nco, 128, nki, 128)
                 .transpose(0, 3, 2, 1).reshape(nco, 128, nki * 128))
    o["wdq"] = img(i["L0_attn_W_dq"], 12, 16)
    o["wuq"] = img(i["L0_attn_W_uq"], 16, 12)
    o["wdkv"] = img(i["L0_attn_W_dkv"], 4, 16)
    o["wuk"] = img(i["L0_attn_W_uk"], 16, 4)
    o["wuv"] = c(np.asarray(i["L0_attn_W_uv"]).reshape(4, 512, 4, 128)
                 .transpose(0, 3, 2, 1).reshape(4, 128, 2048))
    o["wo"] = img(i["L0_attn_W_o"], 16, 16)
    o["expw"] = c(np.asarray(i["L0_ffn_expert_w"])
                  .reshape(E, 16, 128, 16, 128)
                  .transpose(0, 3, 2, 1, 4).reshape(E, 16, 128, 2048))
    Wqr = np.asarray(i["L0_attn_W_qr"])
    o["wqr"] = c(Wqr.T.reshape(12, 128, 64).transpose(1, 0, 2)
                 .reshape(128, 768))
    o["wqrs"] = c(Wqr.reshape(32, 2, DCQ)[:, ::-1, :].reshape(64, DCQ)
                  .T.reshape(12, 128, 64).transpose(1, 0, 2).reshape(128, 768))
    Wkr = np.asarray(i["L0_attn_W_kr"])
    o["wkr"] = c(Wkr.T.reshape(16, 128, 64).transpose(1, 0, 2)
                 .reshape(128, 1024))
    o["wkrs"] = c(Wkr.reshape(32, 2, D)[:, ::-1, :].reshape(64, D)
                  .T.reshape(16, 128, 64).transpose(1, 0, 2).reshape(128, 1024))
    o["gwH"] = c(np.asarray(i["L0_ffn_gate_w"]).reshape(16, 128, 8)
                 .transpose(1, 0, 2).reshape(128, 128))
    o["gbH"] = c(np.asarray(i["L0_ffn_gate_b"]).reshape(8, 1))
    o["ebH"] = c(np.asarray(i["L0_ffn_expert_b"]).reshape(E, 16, 128)
                 .transpose(2, 0, 1).reshape(128, 128))
    o["qnH"] = c(np.asarray(i["L0_attn_q_norm_w"]).reshape(12, 128).T)
    o["kvnH"] = c(np.asarray(i["L0_attn_kv_norm_w"]).reshape(4, 128).T)
    o["identH"] = np.eye(128, dtype=f32)
    o["onesfH"] = np.ones((128, 128), f32)
    o["onesbH"] = np.ones((128, 128), bf)
    fc = np.zeros((128, 12), f32)
    fc[:, 0] = math.log(INV)
    fc[:, 1] = 1e30
    fc[:, 2] = 0.25
    fc[:, 3] = 1e-5
    fc[:, 4] = float(np.finfo(np.float32).eps)
    fc[:, 5] = 1.0 / 2048.0
    fc[:, 6] = 1.0 / 1536.0
    fc[:, 7] = 1.0 / 512.0
    fc[:, 8] = -1.0
    fc[:, 9] = SHIFT
    o["fconsH"] = fc
    hs = np.zeros((32, 2048), bf)
    for cc in range(16):
        hs[2 * cc, cc * 128:cc * 128 + 64] = 1
        hs[2 * cc + 1, cc * 128 + 64:cc * 128 + 128] = 1
    o["hselH"] = hs
    inv = 1.0 / (10000.0 ** (np.arange(0, 64, 2, dtype=f32) / 64.0))
    t = np.arange(T, dtype=f32)
    fr = np.outer(inv, t)
    Ct = np.zeros((64, T), f32)
    St = np.zeros((64, T), f32)
    Ct[0::2] = np.cos(fr)
    Ct[1::2] = np.cos(fr)
    St[0::2] = -np.sin(fr)
    St[1::2] = np.sin(fr)
    o["_C"] = Ct.astype(bf)
    o["_S"] = St.astype(bf)
    return o


def make_in_maps(inputs):
    sh = prep_shared(inputs)
    Cf = sh.pop("_C")
    Sf = sh.pop("_S")
    x = np.asarray(inputs["L0_attn_x"])[0]
    xTfull = np.ascontiguousarray(x.T)
    maps = []
    for r in range(NC):
        m = dict(sh)
        m["xT"] = np.ascontiguousarray(xTfull[:, r * TL:(r + 1) * TL])
        m["ropeCH"] = np.ascontiguousarray(Cf[:, r * TL:(r + 1) * TL])
        m["ropeSH"] = np.ascontiguousarray(Sf[:, r * TL:(r + 1) * TL])
        maps.append(m)
    return maps


def run(inputs, stage=7, sim=False):
    nc, b = build_program(stage)
    nc = finish_build(nc, b)
    maps = make_in_maps(inputs)
    names = ["out"]
    if stage == 8:
        names = ["dbg_k0", "dbg_k1", "dbg_qt", "dbg_pt", "dbg_bias", "dbg_vs",
                 "dbg_ao", "dbg_ao1", "dbg_k2"]
    elif stage < 7:
        names = ["dbg_actA", "dbg_mask", "dbg_xres"]
        if stage >= 4:
            names += ["dbg_cq", "dbg_stg"]
        if stage in (3, 4, 5):
            names.remove("dbg_mask")
    if sim:
        from concourse import bass_interp
        s = bass_interp.MultiCoreSim(nc, NC)
        for r in range(NC):
            for k, v in maps[r].items():
                s.cores[r].tensor(k)[:] = v
        s.simulate()
        return [{n: np.asarray(s.cores[r].mem_tensor(n)) for n in names}
                for r in range(NC)]
    res = run_bass_kernel_spmd(nc, maps, core_ids=list(range(NC)))
    return res.results


def kernel(**inputs):
    res = run(inputs, stage=7, sim=False)
    parts = [np.asarray(r["out"]).T for r in res]
    return np.concatenate(parts, axis=0)[None, :, :].astype(np.float32)
